# version 15
# baseline (speedup 1.0000x reference)
import contextlib
import numpy as np
import ml_dtypes
import concourse.bass as bass
import concourse.mybir as mybir
from concourse.bass_utils import run_bass_kernel_spmd

F32 = mybir.dt.float32
BF16 = mybir.dt.bfloat16
AF = mybir.ActivationFunctionType
AX = mybir.AxisListType
ALU = mybir.AluOpType

S = 4096
D = 2048
NDC = 16
NIC = 44
NH = 16
SCALE = np.float32(1.0 / np.sqrt(128.0))
ESHIFT = np.float32(-8.0)
EPS = 1e-5

ENGS = ("sync", "gpsimd", "tensor", "scalar", "vector")
PHASES = 5
DBG = False


def tmax(*toks):
    out = {}

    def add(t):
        if t is None:
            return
        if isinstance(t, list):
            for u in t:
                add(u)
        else:
            s, v = t
            if v > out.get(s, -1):
                out[s] = v
    for t in toks:
        add(t)
    return [(s, v) for s, v in out.items()] or None


class Builder:
    def __init__(self):
        self.q = {e: [] for e in ENGS}
        self.semnames = []
        self.counts = {}
        self.waited = {e: {} for e in ENGS}
        self.own = {}

    def sem(self, name, owner):
        self.semnames.append(name)
        self.counts[name] = 0
        self.own[name] = owner
        return name

    def wait(self, eng, tok):
        if tok is None:
            return
        if isinstance(tok, list):
            for t in tmax(tok):
                self._wait1(eng, t)
            return
        self._wait1(eng, tok)

    def _wait1(self, eng, tok):
        s, v = tok
        if v <= 0:
            return
        if self.own[s] == eng:
            return
        if self.waited[eng].get(s, -1) >= v:
            return
        self.waited[eng][s] = v
        self.q[eng].append(("w", s, v))

    def wait_own(self, eng, tok):
        s, v = tok
        if v <= 0 or self.waited[eng].get(s, -1) >= v:
            return
        self.waited[eng][s] = v
        self.q[eng].append(("w", s, v))

    def op(self, eng, fn, sem=None, inc=0):
        self.q[eng].append(("o", fn, sem, inc))
        if sem is not None:
            self.counts[sem] += inc
            return (sem, self.counts[sem])
        return None

    def final(self, sem):
        return (sem, self.counts[sem])

    def emit(self, nc, stack):
        hs = {n: stack.enter_context(nc.semaphore(n)) for n in self.semnames}
        with nc.Block() as block:
            for eng in ENGS:
                items = self.q[eng]

                def make(items):
                    def body(e):
                        for it in items:
                            if it[0] == "w":
                                e.wait_ge(hs[it[1]], it[2])
                            else:
                                ins = it[1](e)
                                if it[2] is not None:
                                    ins.then_inc(hs[it[2]], it[3])
                    return body

                getattr(block, eng)(make(items))


def build_nc():
    nc = bass.Bass("TRN2", target_bir_lowering=False)
    b = Builder()
    s_pe = b.sem("s_pe", "tensor")
    s_act = b.sem("s_act", "scalar")
    s_dve = b.sem("s_dve", "vector")
    s_gp = b.sem("s_gp", "gpsimd")
    ds = lambda name, n: [b.sem("%s%d" % (name, i), None) for i in range(n)]
    m_const = b.sem("m_const", None)
    m_x = ds("m_x", 2)
    m_wq = ds("m_wq", 2)
    m_qkst = ds("m_qkst", 2)
    m_vst = ds("m_vst", 2)
    m_head = ds("m_head", 2)
    m_af = ds("m_af", 2)
    m_dwl = ds("m_dwl", 2)
    m_dws = ds("m_dws", 2)
    m_afm = b.sem("m_afm", None)
    m_wo = b.sem("m_wo", None)
    m_y = ds("m_y", 2)
    m_l2st = b.sem("m_l2st", None)
    m_l2f = b.sem("m_l2f", None)
    m_gw = ds("m_gw", 2)
    m_uw = ds("m_uw", 2)
    m_dwp = ds("m_dwp", 4)
    m_ffn = ds("m_ffn", 4)
    m_ft = ds("m_ft", 2)
    m_y5 = ds("m_y5", 2)
    m_out = ds("m_out", 2)
    m_t1 = b.sem("m_t1", None)
    m_t2 = b.sem("m_t2", None)
    m_dbg = b.sem("m_dbg", None)

    x_h = nc.dram_tensor("x", [S, D], F32, kind="ExternalInput").ap()
    wqkv_h = nc.dram_tensor("wqkv_sw", [12, 128, NDC, 512], F32, kind="ExternalInput").ap()
    bqkv_h = nc.dram_tensor("bqkv_t", [128, 48], F32, kind="ExternalInput").ap()
    bvbc_h = nc.dram_tensor("bv_bc", [128, D], F32, kind="ExternalInput").ap()
    wo_h = nc.dram_tensor("wo_n", [D, D], F32, kind="ExternalInput").ap()
    bobc_h = nc.dram_tensor("bo_bc", [128, D], F32, kind="ExternalInput").ap()
    gw_h = nc.dram_tensor("gw_sw", [NIC, 128, NDC, 128], F32, kind="ExternalInput").ap()
    uw_h = nc.dram_tensor("uw_sw", [NIC, 128, NDC, 128], F32, kind="ExternalInput").ap()
    dw_h = nc.dram_tensor("dw_t", [NIC, 128, D], F32, kind="ExternalInput").ap()
    gs_h = nc.dram_tensor("gs_t", [128, NIC], F32, kind="ExternalInput").ap()
    us_h = nc.dram_tensor("us_t", [128, NIC], F32, kind="ExternalInput").ap()
    dsbc_h = nc.dram_tensor("ds_bc", [128, D], F32, kind="ExternalInput").ap()
    mask_h = nc.dram_tensor("masks", [128, 2048], BF16, kind="ExternalInput").ap()
    idf_h = nc.dram_tensor("ident_f32", [128, 128], F32, kind="ExternalInput").ap()
    onef_h = nc.dram_tensor("ones_f32", [1, 128], F32, kind="ExternalInput").ap()
    out_h = nc.dram_tensor("out", [S, D], F32, kind="ExternalOutput").ap()

    KI = "ExternalOutput" if DBG else "Internal"
    q_sc = nc.dram_tensor("q_sc", [NH, 128, S], BF16, kind=KI).ap()
    k_sc = nc.dram_tensor("k_sc", [NH, 128, S], BF16, kind=KI).ap()
    v_sc = nc.dram_tensor("v_sc", [NH, 32, 128, 128], BF16, kind=KI).ap()
    af_sc = nc.dram_tensor("af_sc", [NDC, 128, S], BF16, kind=KI).ap()
    l2_sc = nc.dram_tensor("l2_sc", [NDC, 128, S], BF16, kind=KI).ap()
    dwbf_sc = nc.dram_tensor("dwbf_sc", [NIC, 128, D], BF16, kind=KI).ap()
    ffn_sc = nc.dram_tensor("ffn_sc", [S, D], F32, kind=KI).ap()
    hdn_dbg = nc.dram_tensor("hdn_dbg", [128, 45056], BF16, kind=KI).ap() if DBG else None

    pbf = nc.alloc_sbuf_tensor("pool_bf", [128, 71680], BF16).ap()
    pf = nc.alloc_sbuf_tensor("pool_f32", [128, 14336], F32).ap()
    mask_sb = nc.alloc_sbuf_tensor("mask_sb", [128, 2048], BF16).ap()
    idf_sb = nc.alloc_sbuf_tensor("idf_sb", [128, 128], F32).ap()
    onef_sb = nc.alloc_sbuf_tensor("onef_sb", [1, 128], F32).ap()
    bqkv_sb = nc.alloc_sbuf_tensor("bqkv_sb", [128, 48], F32).ap()
    gs_sb = nc.alloc_sbuf_tensor("gs_sb", [128, NIC], F32).ap()
    us_sb = nc.alloc_sbuf_tensor("us_sb", [128, NIC], F32).ap()
    st_sb = nc.alloc_sbuf_tensor("st_sb", [128, 16], F32).ap()
    iz_sb = nc.alloc_sbuf_tensor("iz_sb", [1, 1024], F32).ap()
    zbc_sb = nc.alloc_sbuf_tensor("zbc_sb", [128, 512], F32).ap()

    ps = [nc.alloc_psum_tensor("ps%d" % i, [128, 512], F32).ap() for i in range(8)]
    bankfree = [None] * 8

    b.op("sync", lambda e: e.dma_start(mask_sb[:], mask_h), m_const, 16)
    b.op("sync", lambda e: e.dma_start(idf_sb[:], idf_h), m_const, 16)
    b.op("sync", lambda e: e.dma_start(onef_sb[:], onef_h), m_const, 16)
    b.op("sync", lambda e: e.dma_start(bqkv_sb[:], bqkv_h), m_const, 16)
    b.op("sync", lambda e: e.dma_start(gs_sb[:], gs_h), m_const, 16)
    tok_const = b.op("sync", lambda e: e.dma_start(us_sb[:], us_h), m_const, 16)

    def emit_ln(c, tile, sq_off):
        b.op("vector", (lambda c, tile: lambda e: e.reduce_sum(st_sb[:, c:c + 1], tile, axis=AX.X))(c, tile), s_dve, 1)
        t2 = b.op("vector", (lambda c, tile, sq_off: lambda e: e.scalar_tensor_tensor(
            pf[:, sq_off:sq_off + D], tile, 1.0, tile, ALU.mult, ALU.mult,
            accum_out=st_sb[:, c + 1:c + 2]))(c, tile, sq_off), s_dve, 1)
        b.wait_own("vector", t2)
        b.op("vector", (lambda c: lambda e: e.tensor_scalar(st_sb[:, c:c + 1], st_sb[:, c:c + 1], 1.0 / D, 0.0, ALU.mult, ALU.add))(c), s_dve, 1)
        t4 = b.op("vector", (lambda c: lambda e: e.tensor_scalar(st_sb[:, c + 1:c + 2], st_sb[:, c + 1:c + 2], 1.0 / D, EPS, ALU.mult, ALU.add))(c), s_dve, 1)
        b.wait_own("vector", t4)
        t5 = b.op("vector", (lambda c: lambda e: e.tensor_tensor(st_sb[:, c + 2:c + 3], st_sb[:, c:c + 1], st_sb[:, c:c + 1], ALU.mult))(c), s_dve, 1)
        b.op("vector", (lambda c, tile: lambda e: e.tensor_scalar(tile, tile, st_sb[:, c:c + 1], 0.0, ALU.subtract, ALU.add))(c, tile))
        b.wait_own("vector", t5)
        t7 = b.op("vector", (lambda c: lambda e: e.tensor_tensor(st_sb[:, c + 1:c + 2], st_sb[:, c + 1:c + 2], st_sb[:, c + 2:c + 3], ALU.subtract))(c), s_dve, 1)
        b.wait("scalar", t7)
        t8 = b.op("scalar", (lambda c: lambda e: e.activation(st_sb[:, c + 2:c + 3], st_sb[:, c + 1:c + 2], AF.Ln, bias=0.0, scale=np.float32(1.0)))(c), s_act, 1)
        b.wait_own("scalar", t8)
        ta = b.op("scalar", (lambda c: lambda e: e.activation(st_sb[:, c + 3:c + 4], st_sb[:, c + 2:c + 3], AF.Exp, bias=0.0, scale=np.float32(-0.5)))(c), s_act, 1)
        return ta

    ln1fm = [pbf[:, 0:16384], pbf[:, 16384:32768]]
    wqp = [pbf[:, 32768:40960], pbf[:, 40960:49152]]
    qkst = [pbf[:, 49152:51200], pbf[:, 51200:53248]]
    vst = [pbf[:, 53248:53760], pbf[:, 53760:54272]]
    xt = [pf[:, 0:2048], pf[:, 2048:4096]]
    ln1f = pf[:, 4096:6144]
    bvbc_sb = pf[:, 8192:10240]

    t_bv = b.op("sync", lambda e: e.dma_start(bvbc_sb, bvbc_h), m_const, 16)
    b.wait("tensor", t_bv)
    b.wait("vector", t_bv)
    b.wait("scalar", t_bv)

    xt_free = [None, None]
    wqp_free = [None, None]
    qkst_free = [None, None]
    vst_free = [None, None]
    ln1fm_ready = [[None, None], [None, None]]
    ln1fm_free = [None, None]
    ln1f_free = [None]

    def emit_ln_tile(p, j):
        par = p % 2
        tj = p * 8 + j
        xb = j % 2
        b.wait("sync", xt_free[xb])
        tl = b.op("sync", (lambda tj, xb: lambda e: e.dma_start(xt[xb], x_h[tj * 128:(tj + 1) * 128, :]))(tj, xb), m_x[xb], 16)
        b.wait("vector", tmax([tl]))
        ta = emit_ln(0, xt[xb], 6144)
        b.wait("vector", tmax([ta], [ln1f_free[0]]))
        tv = b.op("vector", (lambda xb: lambda e: e.tensor_scalar(ln1f, xt[xb], st_sb[:, 3:4], 0.0, ALU.mult, ALU.add))(xb), s_dve, 1)
        xt_free[xb] = tv
        last_evac = None
        for qd in range(4):
            bk = 4 + qd
            b.wait("tensor", tmax([tv], [bankfree[bk]], [ln1fm_free[par]] if j == 0 and qd == 0 else None))
            for w in range(4):
                dc = qd * 4 + w
                tk = b.op("tensor", (lambda bk, w, dc: lambda e: e.transpose(
                    ps[bk][:, w * 128:(w + 1) * 128], ln1f[:, dc * 128:(dc + 1) * 128], idf_sb))(bk, w, dc),
                    s_pe if w == 3 else None, 1 if w == 3 else 0)
            ln1f_free[0] = tk
            b.wait("vector", tmax([tk]))
            dst = ln1fm[par].rearrange("p (dc t) -> p dc t", t=1024)[:, qd * 4:(qd + 1) * 4, j * 128:(j + 1) * 128]
            srcv = ps[bk].rearrange("p (w t) -> p w t", t=128)
            te = b.op("vector", (lambda dst, srcv: lambda e: e.tensor_copy(dst, srcv))(dst, srcv), s_dve, 1)
            bankfree[bk] = te
            last_evac = te
        if j == 3:
            ln1fm_ready[par][0] = last_evac
        if j == 7:
            ln1fm_ready[par][1] = last_evac

    for j in range(8):
        emit_ln_tile(0, j)

    for p in range(4):
        par = p % 2
        mmtok = None
        for pn in range(12):
            wb = pn % 2
            b.wait("gpsimd", tmax([wqp_free[wb]], [tok_const] if p == 0 and pn < 2 else None))
            tw = b.op("gpsimd", (lambda pn, wb: lambda e: e.dma_start(
                wqp[wb].rearrange("p (dc o) -> p dc o", o=512), wqkv_h[pn]))(pn, wb), m_wq[wb], 16)
            if pn < 8:
                for k in range(2):
                    sb = (pn * 2 + k) % 2
                    evtoks = []
                    for oc in range(4):
                        bk = (pn * 8 + k * 4 + oc) % 4
                        b.wait("tensor", tmax([tw], [ln1fm_ready[par][k]], [bankfree[bk]]))
                        for dc in range(NDC):
                            tk = b.op("tensor", (lambda wb, dc, oc, bk, par, k: lambda e: e.matmul(
                                ps[bk], wqp[wb][:, dc * 512 + oc * 128: dc * 512 + (oc + 1) * 128],
                                ln1fm[par][:, dc * 1024 + k * 512: dc * 1024 + (k + 1) * 512],
                                start=(dc == 0), stop=(dc == NDC - 1)))(wb, dc, oc, bk, par, k),
                                s_pe if dc == NDC - 1 else None, 1 if dc == NDC - 1 else 0)
                        mmtok = tk
                        cglob = pn * 4 + oc
                        b.wait("scalar", tmax([tk], [qkst_free[sb]] if oc == 0 else None))
                        tac = b.op("scalar", (lambda sb, oc, bk, cglob: lambda e: e.activation(
                            qkst[sb][:, oc * 512:(oc + 1) * 512], ps[bk], AF.Identity,
                            bias=bqkv_sb[:, cglob:cglob + 1]))(sb, oc, bk, cglob), s_act, 1)
                        bankfree[bk] = tac
                        evtoks.append(tac)
                    b.wait("sync", evtoks)
                    dest = q_sc if pn < 4 else k_sc
                    h0 = (pn * 4) % 16
                    t0 = p * 1024 + k * 512
                    tst = b.op("sync", (lambda dest, h0, t0, sb: lambda e: e.dma_start(
                        dest[h0:h0 + 4, :, t0:t0 + 512].rearrange("h p t -> p h t"),
                        qkst[sb].rearrange("p (h t) -> p h t", h=4)))(dest, h0, t0, sb), m_qkst[sb], 16)
                    qkst_free[sb] = tst
            else:
                for j in range(8):
                    tj = p * 8 + j
                    sb = (pn * 8 + j) % 2
                    bk = (pn * 8 + j) % 4
                    b.wait("tensor", tmax([tw], [ln1fm_ready[par][j // 4]], [bankfree[bk]]))
                    for dc in range(NDC):
                        tk = b.op("tensor", (lambda wb, dc, j, bk, par: lambda e: e.matmul(
                            ps[bk], ln1fm[par][:, dc * 1024 + j * 128: dc * 1024 + (j + 1) * 128],
                            wqp[wb][:, dc * 512:(dc + 1) * 512],
                            start=(dc == 0), stop=(dc == NDC - 1)))(wb, dc, j, bk, par),
                            s_pe if dc == NDC - 1 else None, 1 if dc == NDC - 1 else 0)
                    mmtok = tk
                    b.wait("vector", tmax([tk], [vst_free[sb]], [t_bv]))
                    tvv = b.op("vector", (lambda sb, bk, pn: lambda e: e.scalar_tensor_tensor(
                        vst[sb], ps[bk], 1.0, bvbc_sb[:, (pn - 8) * 512:(pn - 7) * 512],
                        ALU.mult, ALU.add))(sb, bk, pn), s_dve, 1)
                    bankfree[bk] = tvv
                    b.wait("sync", tvv)
                    h0 = (pn - 8) * 4
                    tsv = b.op("sync", (lambda sb, h0, tj: lambda e: e.dma_start(
                        v_sc[h0:h0 + 4, tj, :, :].rearrange("h p w -> p h w"),
                        vst[sb].rearrange("p (h w) -> p h w", w=128)))(sb, h0, tj), m_vst[sb], 16)
                    vst_free[sb] = tsv
            wqp_free[wb] = mmtok
            if p < 3 and 4 <= pn < 12:
                emit_ln_tile(p + 1, pn - 4)
        ln1fm_free[par] = mmtok

    tok_p1_sync = tmax([b.final(s) for s in m_qkst + m_vst])
    tok_p1_pe = (s_pe, b.counts["s_pe"])

    if PHASES < 2:
        return _finish(nc, b)

    qb = [pbf[:, 0:4096], pbf[:, 4096:8192]]
    kb = [pbf[:, 8192:12288], pbf[:, 12288:16384]]
    pbuf = [pbf[:, 16384:32768], pbf[:, 46336:62720]]
    va_view = [pbf[:, 32768 + k * 4224: 32768 + (k + 1) * 4224].rearrange("p (c w) -> p c w", w=132) for k in range(2)]
    afst = [pbf[:, 41216:41728], pbf[:, 41728:42240]]
    dwc = [pbf[:, 42240:44288], pbf[:, 44288:46336]]

    b.wait("vector", tok_p1_pe)
    tok_ones = []
    for k2 in range(2):
        tok_ones.append(b.op("vector", (lambda k2: lambda e: e.memset(va_view[k2][:, :, 128:129], 1.0))(k2), s_dve, 1))

    pe_head_last = [tok_p1_pe, tok_p1_pe]
    afst_free = [tok_p1_pe, tok_p1_pe]
    pbuf_free = [None, None]
    iz_free = [None, None]
    dwc_store = [None, None]
    dw_ip = [0]
    head_loads = [None, None]

    def emit_head_loads(h):
        hb = h % 2
        b.wait("sync", tmax(tok_p1_sync, [pe_head_last[hb]], tok_ones if h < 2 else None))
        b.op("sync", (lambda h, hb: lambda e: e.dma_start(qb[hb], q_sc[h]))(h, hb), m_head[hb], 16)
        b.op("sync", (lambda h, hb: lambda e: e.dma_start(kb[hb], k_sc[h]))(h, hb), m_head[hb], 16)
        ld = None
        for vq in range(4):
            ld = b.op("sync", (lambda h, hb, vq: lambda e: e.dma_start(
                va_view[hb][:, vq * 8:(vq + 1) * 8, 0:128],
                v_sc[h][vq * 8:(vq + 1) * 8].rearrange("c p w -> p c w")))(h, hb, vq), m_head[hb], 16)
        head_loads[hb] = ld
        for _ in range(3):
            ip = dw_ip[0]
            if ip < NIC:
                cb = ip % 2
                b.wait("gpsimd", tmax([tok_p1_pe] if ip < 2 else None, [dwc_store[cb]]))
                tld = b.op("gpsimd", (lambda ip, cb: lambda e: e.dma_start(dwc[cb], dw_h[ip]))(ip, cb), m_dwl[cb], 16)
                b.wait("sync", tld)
                dwc_store[cb] = b.op("sync", (lambda ip, cb: lambda e: e.dma_start(dwbf_sc[ip], dwc[cb]))(ip, cb), m_dws[cb], 16)
                dw_ip[0] += 1

    def emit_pv_chunk(u, c):
        hb, par, nk = u["hb"], u["par"], u["nk"]
        bo_, bz = 2 + 3 * par, 3 + 3 * par
        b.wait("tensor", tmax([u["exp"][c]],
                              [bankfree[bo_], bankfree[bz]] if c == 0 else None))
        pb = pbuf[par]
        b.op("tensor", (lambda hb, c, st, sp, bo_, pb: lambda e: e.matmul(
            ps[bo_], va_view[hb][:, c, 0:128], pb[:, c * 512:(c + 1) * 512],
            start=st, stop=sp))(hb, c, c == 0, c == nk - 1, bo_, pb))
        tz = b.op("tensor", (lambda hb, c, st, sp, bz, pb: lambda e: e.matmul(
            ps[bz][0:1, :], va_view[hb][:, c, 128:129], pb[:, c * 512:(c + 1) * 512],
            start=st, stop=sp))(hb, c, c == 0, c == nk - 1, bz, pb), s_pe, 1)
        return tz

    def emit_recip(u, pv_last):
        par = u["par"]
        bz = 3 + 3 * par
        b.wait("vector", tmax([pv_last], [iz_free[par]]))
        trc = b.op("vector", (lambda bz, par: lambda e: e.reciprocal(
            iz_sb[0:1, par * 512:(par + 1) * 512], ps[bz][0:1, :]))(bz, par), s_dve, 1)
        bankfree[bz] = trc
        u["trc"] = trc

    def emit_evacs(u):
        par = u["par"]
        bo_, bb_ = 2 + 3 * par, 4 + 3 * par
        b.wait("tensor", tmax([u["trc"]], [bankfree[bb_]]))
        tbc = b.op("tensor", (lambda bb_, par: lambda e: e.matmul(
            ps[bb_], onef_sb, iz_sb[0:1, par * 512:(par + 1) * 512], start=True, stop=True))(bb_, par), s_pe, 1)
        iz_free[par] = tbc
        b.wait("vector", tbc)
        tcp = b.op("vector", (lambda bb_: lambda e: e.tensor_copy(zbc_sb, ps[bb_]))(bb_), s_dve, 1)
        bankfree[bb_] = tcp
        ab = u["ab"]
        b.wait("vector", afst_free[ab])
        tstt = b.op("vector", (lambda bo_, ab: lambda e: e.scalar_tensor_tensor(
            afst[ab], ps[bo_], 1.0, zbc_sb, ALU.mult, ALU.mult))(bo_, ab), s_dve, 1)
        bankfree[bo_] = tstt
        b.wait("sync", tstt)
        tsa = b.op("sync", (lambda u, ab: lambda e: e.dma_start(
            af_sc[u["h"], :, u["tq"] * 512:(u["tq"] + 1) * 512], afst[ab]))(u, ab), m_af[ab], 16)
        afst_free[ab] = tsa

    prev = None
    pend_evac = []
    for i in range(NH * 8):
        h, tq = i // 8, i % 8
        if i == 0:
            emit_head_loads(0)
        if tq == 1 and h + 1 < NH:
            emit_head_loads(h + 1)
        hb = h % 2
        par = i % 2
        nk = 4 * (tq + 1)
        u = {"h": h, "tq": tq, "hb": hb, "par": par, "nk": nk, "exp": {}, "ab": i % 2}
        if len(pend_evac) == 1:
            emit_evacs(pend_evac.pop(0))
        prev_nk = prev["nk"] if prev else 0
        pv_last = None
        for c in range(max(nk, prev_nk)):
            if c < nk:
                bk = c % 2
                b.wait("tensor", tmax([head_loads[hb]], [bankfree[bk]]))
                tsc = b.op("tensor", (lambda hb, c, tq, bk: lambda e: e.matmul(
                    ps[bk], kb[hb][:, c * 128:(c + 1) * 128], qb[hb][:, tq * 512:(tq + 1) * 512],
                    start=True, stop=True))(hb, c, tq, bk), s_pe, 1)
                b.wait("scalar", tmax([tsc], [pbuf_free[par]] if c == 0 else None))
                te = b.op("scalar", (lambda c, bk, par: lambda e: e.activation(
                    pbuf[par][:, c * 512:(c + 1) * 512], ps[bk], AF.Exp, bias=ESHIFT, scale=SCALE))(c, bk, par), s_act, 1)
                bankfree[bk] = te
                if c >= 4 * tq:
                    kk = c - 4 * tq
                    b.wait("gpsimd", te)
                    te = b.op("gpsimd", (lambda c, kk, par: lambda e: e.tensor_tensor(
                        pbuf[par][:, c * 512:(c + 1) * 512], pbuf[par][:, c * 512:(c + 1) * 512],
                        mask_sb[:, kk * 512:(kk + 1) * 512], ALU.mult))(c, kk, par), s_gp, 1)
                u["exp"][c] = te
            if prev is not None and c < prev_nk:
                pv_last = emit_pv_chunk(prev, c)
        if prev is not None:
            pbuf_free[prev["par"]] = pv_last
            if prev["tq"] == 7:
                pe_head_last[prev["hb"]] = pv_last
            emit_recip(prev, pv_last)
            pend_evac.append(prev)
        prev = u
    pv_last = None
    for c in range(prev["nk"]):
        pv_last = emit_pv_chunk(prev, c)
    pbuf_free[prev["par"]] = pv_last
    pe_head_last[prev["hb"]] = pv_last
    emit_recip(prev, pv_last)
    pend_evac.append(prev)
    for uu in pend_evac:
        emit_evacs(uu)

    tok_p2_sync = tmax([b.final(s) for s in m_af])
    tok_p2_pe = (s_pe, b.counts["s_pe"])
    tok_p2_act = (s_act, b.counts["s_act"])
    tok_p2_dve = (s_dve, b.counts["s_dve"])
    tok_dwprep = tmax(dwc_store)

    if PHASES < 3:
        return _finish(nc, b)

    afm = pbf[:, 0:16384]
    wo_sb = pbf[:, 16384:49152]
    l2st = pbf[:, 49152:57344]
    projs = [pf[:, 4096:6144], pf[:, 10240:12288]]
    yt = [pf[:, 6144:8192], pf[:, 8192:10240]]
    sqo = [6144, 8192]
    bobc_sb = pf[:, 12288:14336]
    yt_free = [tok_p2_dve, tok_p2_dve]

    twl = None
    for dc in range(NDC):
        b.wait("gpsimd", tmax([tok_p2_pe], [tok_p2_act], [tok_p2_dve], tok_dwprep))
        twl = b.op("gpsimd", (lambda dc: lambda e: e.dma_start(
            wo_sb[:, dc * 2048:(dc + 1) * 2048], wo_h[dc * 128:(dc + 1) * 128, :]))(dc), m_wo, 16)
    tok_wo = twl
    b.wait("sync", tmax([tok_p2_pe], [tok_p2_dve]))
    t_bo = b.op("sync", lambda e: e.dma_start(bobc_sb, bobc_h), m_const, 16)

    afm_free = tok_p2_pe
    l2st_free = None
    xt_free3 = [xt_free[0], xt_free[1]]
    proj_free = [None, None]
    l2_state = {"free": None}

    def emit_l2_transp(jj, p, tv2_tok):
        last = None
        proj = projs[jj % 2]
        for qd in range(4):
            bk = 4 + qd
            b.wait("tensor", tmax([tv2_tok], [bankfree[bk]]))
            for w in range(4):
                dc = qd * 4 + w
                tk2 = b.op("tensor", (lambda bk, w, dc, proj: lambda e: e.transpose(
                    ps[bk][:, w * 128:(w + 1) * 128], proj[:, dc * 128:(dc + 1) * 128], idf_sb))(bk, w, dc, proj),
                    s_pe if w == 3 else None, 1 if w == 3 else 0)
            last = tk2
            b.wait("vector", tmax([tk2], [l2_state["free"]] if qd == 0 and jj % 4 == 0 else None))
            dst = l2st.rearrange("p (hc t) -> p hc t", t=512)[:, qd * 4:(qd + 1) * 4, (jj % 4) * 128:((jj % 4) + 1) * 128]
            srcv = ps[bk].rearrange("p (w t) -> p w t", t=128)
            te2 = b.op("vector", (lambda dst, srcv: lambda e: e.tensor_copy(dst, srcv))(dst, srcv), s_dve, 1)
            bankfree[bk] = te2
        if jj % 4 == 3:
            b.wait("sync", te2)
            t0 = p * 1024 + (jj // 4) * 512
            fl = None
            for hc in range(NDC):
                fl = b.op("sync", (lambda hc, t0: lambda e: e.dma_start(
                    l2_sc[hc][:, t0:t0 + 512], l2st[:, hc * 512:(hc + 1) * 512]))(hc, t0), m_l2st, 16)
            l2_state["free"] = fl
        return last

    for p in range(4):
        b.wait("sync", tmax([afm_free], tok_p2_sync))
        tok_afm = None
        for hc in range(NDC):
            tok_afm = b.op("sync", (lambda hc, p: lambda e: e.dma_start(
                afm[:, hc * 1024:(hc + 1) * 1024], af_sc[hc][:, p * 1024:(p + 1) * 1024]))(hc, p), m_afm, 16)
        mm_last = None
        pend = None
        for j in range(8):
            tj = p * 8 + j
            xb = j % 2
            proj = projs[j % 2]
            b.wait("sync", xt_free3[xb])
            tlx = b.op("sync", (lambda tj, xb: lambda e: e.dma_start(xt[xb], x_h[tj * 128:(tj + 1) * 128, :]))(tj, xb), m_x[xb], 16)
            for o in range(4):
                bk = (j * 4 + o) % 4
                b.wait("tensor", tmax([tok_afm], [tok_wo], [bankfree[bk]]))
                for dc in range(NDC):
                    tk = b.op("tensor", (lambda dc, j, o, bk: lambda e: e.matmul(
                        ps[bk], afm[:, dc * 1024 + j * 128: dc * 1024 + (j + 1) * 128],
                        wo_sb[:, dc * 2048 + o * 512: dc * 2048 + (o + 1) * 512],
                        start=(dc == 0), stop=(dc == NDC - 1)))(dc, j, o, bk),
                        s_pe if dc == NDC - 1 else None, 1 if dc == NDC - 1 else 0)
                mm_last = tk
                b.wait("vector", tmax([tk], [t_bo], [proj_free[j % 2]] if o == 0 else None,
                                      [yt_free[1 - xb]] if o == 0 else None))
                tvp = b.op("vector", (lambda o, bk, proj: lambda e: e.scalar_tensor_tensor(
                    proj[:, o * 512:(o + 1) * 512], ps[bk], 1.0, bobc_sb[:, o * 512:(o + 1) * 512],
                    ALU.mult, ALU.add, accum_out=st_sb[:, 12 + o:13 + o]))(o, bk, proj), s_dve, 1)
                bankfree[bk] = tvp
            sq_t = pf[:, sqo[1 - xb]:sqo[1 - xb] + D]
            b.wait_own("vector", tvp)
            t_c1 = b.op("vector", lambda e: e.tensor_tensor(st_sb[:, 12:13], st_sb[:, 12:13], st_sb[:, 13:14], ALU.add), s_dve, 1)
            t_c2 = b.op("vector", lambda e: e.tensor_tensor(st_sb[:, 14:15], st_sb[:, 14:15], st_sb[:, 15:16], ALU.add), s_dve, 1)
            t_sq = b.op("vector", (lambda proj, sq_t: lambda e: e.scalar_tensor_tensor(
                sq_t, proj, 1.0, proj, ALU.mult, ALU.mult, accum_out=st_sb[:, 5:6]))(proj, sq_t), s_dve, 1)
            b.wait_own("vector", t_sq)
            t_c3 = b.op("vector", lambda e: e.tensor_tensor(st_sb[:, 4:5], st_sb[:, 12:13], st_sb[:, 14:15], ALU.add), s_dve, 1)
            t_v1 = b.op("vector", lambda e: e.tensor_scalar(st_sb[:, 5:6], st_sb[:, 5:6], 1.0 / D, EPS, ALU.mult, ALU.add), s_dve, 1)
            b.wait_own("vector", t_c3)
            t_m = b.op("vector", lambda e: e.tensor_scalar(st_sb[:, 4:5], st_sb[:, 4:5], 1.0 / D, 0.0, ALU.mult, ALU.add), s_dve, 1)
            b.wait_own("vector", t_m)
            t_m2 = b.op("vector", lambda e: e.tensor_tensor(st_sb[:, 6:7], st_sb[:, 4:5], st_sb[:, 4:5], ALU.mult), s_dve, 1)
            b.op("vector", (lambda proj: lambda e: e.tensor_scalar(proj, proj, st_sb[:, 4:5], 0.0, ALU.subtract, ALU.add))(proj))
            b.wait_own("vector", t_m2)
            t_var = b.op("vector", lambda e: e.tensor_tensor(st_sb[:, 5:6], st_sb[:, 5:6], st_sb[:, 6:7], ALU.subtract), s_dve, 1)
            b.wait("scalar", t_var)
            t_l1 = b.op("scalar", lambda e: e.activation(st_sb[:, 6:7], st_sb[:, 5:6], AF.Ln, bias=0.0, scale=np.float32(1.0)), s_act, 1)
            b.wait_own("scalar", t_l1)
            ta = b.op("scalar", lambda e: e.activation(st_sb[:, 7:8], st_sb[:, 6:7], AF.Exp, bias=0.0, scale=np.float32(-0.5)), s_act, 1)
            b.wait("vector", tmax([ta], [tlx], [yt_free[xb]]))
            ty = b.op("vector", (lambda xb, proj: lambda e: e.scalar_tensor_tensor(
                yt[xb], proj, st_sb[:, 7:8], xt[xb], ALU.mult, ALU.add,
                accum_out=st_sb[:, 8:9]))(xb, proj), s_dve, 1)
            xt_free3[xb] = ty
            b.wait("sync", ty)
            tsy = b.op("sync", (lambda tj, xb: lambda e: e.dma_start(
                out_h[tj * 128:(tj + 1) * 128, :], yt[xb]))(tj, xb), m_y[xb], 16)
            yt_free[xb] = tsy
            t_sq2 = b.op("vector", (lambda xb, sq_t: lambda e: e.scalar_tensor_tensor(
                sq_t, yt[xb], 1.0, yt[xb], ALU.mult, ALU.mult, accum_out=st_sb[:, 9:10]))(xb, sq_t), s_dve, 1)
            b.wait_own("vector", t_sq2)
            t_n1 = b.op("vector", lambda e: e.tensor_scalar(st_sb[:, 8:9], st_sb[:, 8:9], 1.0 / D, 0.0, ALU.mult, ALU.add), s_dve, 1)
            t_n2 = b.op("vector", lambda e: e.tensor_scalar(st_sb[:, 9:10], st_sb[:, 9:10], 1.0 / D, EPS, ALU.mult, ALU.add), s_dve, 1)
            b.wait_own("vector", t_n2)
            t_n3 = b.op("vector", lambda e: e.tensor_tensor(st_sb[:, 10:11], st_sb[:, 8:9], st_sb[:, 8:9], ALU.mult), s_dve, 1)
            b.wait_own("vector", t_n3)
            t_n4 = b.op("vector", lambda e: e.tensor_tensor(st_sb[:, 9:10], st_sb[:, 9:10], st_sb[:, 10:11], ALU.subtract), s_dve, 1)
            b.wait("scalar", t_n4)
            t_l2 = b.op("scalar", lambda e: e.activation(st_sb[:, 10:11], st_sb[:, 9:10], AF.Ln, bias=0.0, scale=np.float32(1.0)), s_act, 1)
            b.wait_own("scalar", t_l2)
            t_e2 = b.op("scalar", lambda e: e.activation(st_sb[:, 11:12], st_sb[:, 10:11], AF.Exp, bias=0.0, scale=np.float32(-0.5)), s_act, 1)
            b.wait("vector", t_e2)
            t_b = b.op("vector", lambda e: e.scalar_tensor_tensor(
                st_sb[:, 10:11], st_sb[:, 8:9], -1.0, st_sb[:, 11:12], ALU.mult, ALU.mult), s_dve, 1)
            b.wait_own("vector", t_b)
            tv2 = b.op("vector", (lambda xb, proj: lambda e: e.tensor_scalar(
                proj, yt[xb], st_sb[:, 11:12], st_sb[:, 10:11], ALU.mult, ALU.add))(xb, proj), s_dve, 1)
            if pend is not None:
                proj_free[pend[0] % 2] = emit_l2_transp(pend[0], p, pend[1])
            pend = (j, tv2)
        proj_free[pend[0] % 2] = emit_l2_transp(pend[0], p, pend[1])
        afm_free = mm_last
    l2st_free = l2_state["free"]

    tok_p3_ysync = tmax([b.final(s) for s in m_y])
    tok_p3_pe = (s_pe, b.counts["s_pe"])
    tok_p3_dve = (s_dve, b.counts["s_dve"])
    tok_p3_act = (s_act, b.counts["s_act"])

    if PHASES < 4:
        return _finish(nc, b)

    hdn = pbf[:, 0:45056]
    ln2fm = pbf[:, 45056:61440]
    gwp = [pbf[:, 61440:63488], pbf[:, 63488:65536]]
    uwp = [pbf[:, 65536:67584], pbf[:, 67584:69632]]
    dwp = [pbf[:, 69632 + i * 512: 69632 + (i + 1) * 512] for i in range(4)]
    silu_sb = [pf[:, i * 512:(i + 1) * 512] for i in range(4)]
    dsbc_sb = pf[:, 2048:4096]
    ffnst = [pf[:, 4096 + i * 512: 4096 + (i + 1) * 512] for i in range(4)]

    b.wait("sync", tmax([tok_p3_pe], [tok_p3_dve]))
    t_ds = b.op("sync", lambda e: e.dma_start(dsbc_sb, dsbc_h), m_const, 16)

    gwp_free = [tok_p3_pe, tok_p3_pe]
    uwp_free = [tok_p3_pe, tok_p3_pe]
    dwp_free = [None, None, None, None]
    ln2fm_free = tok_p3_pe
    ffnst_free = [tok_p3_pe] * 4
    hdn_free = None

    ft = [pf[:, 6144:8192], pf[:, 8192:10240]]
    ft_free = [tok_p3_dve, tok_p3_dve]
    ffn_done_tok = [None]

    def emit_epilogue_tile(pp, j):
        g = pp * 8 + j
        fb = j % 2
        b.wait("sync", tmax([ft_free[fb]], [b.final(s) for s in m_ffn] if j % 2 == 0 else None,
                            tok_p3_ysync if pp == 0 and j == 0 else None))
        tf = b.op("sync", (lambda g, fb: lambda e: e.dma_start(ft[fb], ffn_sc[g * 128:(g + 1) * 128, :]))(g, fb), m_ft[fb], 16)
        b.wait("vector", tf)
        ta5 = emit_ln(12, ft[fb], 10240)
        b.wait("vector", ta5)
        tapp = b.op("vector", (lambda fb: lambda e: e.tensor_scalar(ft[fb], ft[fb], st_sb[:, 15:16], 0.0, ALU.mult, ALU.add))(fb), s_dve, 1)
        b.wait("gpsimd", tmax([tapp], tok_p3_ysync if pp == 0 and j == 0 else None))
        tacc = b.op("gpsimd", (lambda g, fb: lambda e: e.dma_start(
            out_h[g * 128:(g + 1) * 128, :], ft[fb], accum_op=ALU.add))(g, fb), m_out[fb], 16)
        ft_free[fb] = tacc

    for p in range(4):
        b.wait("sync", tmax([ln2fm_free], [l2st_free]))
        tok_l2 = None
        for hc in range(NDC):
            tok_l2 = b.op("sync", (lambda hc, p: lambda e: e.dma_start(
                ln2fm[:, hc * 1024:(hc + 1) * 1024], l2_sc[hc][:, p * 1024:(p + 1) * 1024]))(hc, p), m_l2f, 16)
        hdn_last = None
        gu_last = None
        for ip in range(NIC):
            wb = ip % 2
            b.wait("gpsimd", gwp_free[wb])
            tg = b.op("gpsimd", (lambda ip, wb: lambda e: e.dma_start(
                gwp[wb].rearrange("p (hc i) -> p hc i", i=128), gw_h[ip]))(ip, wb), m_gw[wb], 16)
            b.wait("gpsimd", uwp_free[wb])
            tu = b.op("gpsimd", (lambda ip, wb: lambda e: e.dma_start(
                uwp[wb].rearrange("p (hc i) -> p hc i", i=128), uw_h[ip]))(ip, wb), m_uw[wb], 16)
            bbase = (ip % 2) * 4
            b.wait("tensor", tmax([tg], [tu], [tok_l2], [hdn_free] if ip == 0 else None,
                                  [bankfree[bbase]], [bankfree[bbase + 1]],
                                  [bankfree[bbase + 2]], [bankfree[bbase + 3]]))
            tk = None
            for hc in range(NDC):
                stt_ = hc == 0
                sp = hc == NDC - 1
                for k in range(2):
                    b.op("tensor", (lambda wb, hc, k, bbase, stt_, sp: lambda e: e.matmul(
                        ps[bbase + k], gwp[wb][:, hc * 128:(hc + 1) * 128],
                        ln2fm[:, hc * 1024 + k * 512: hc * 1024 + (k + 1) * 512],
                        start=stt_, stop=sp))(wb, hc, k, bbase, stt_, sp))
                for k in range(2):
                    last = sp and k == 1
                    tk = b.op("tensor", (lambda wb, hc, k, bbase, stt_, sp: lambda e: e.matmul(
                        ps[bbase + 2 + k], uwp[wb][:, hc * 128:(hc + 1) * 128],
                        ln2fm[:, hc * 1024 + k * 512: hc * 1024 + (k + 1) * 512],
                        start=stt_, stop=sp))(wb, hc, k, bbase, stt_, sp),
                        s_pe if last else None, 1 if last else 0)
            gu_last = tk
            gwp_free[wb] = tk
            uwp_free[wb] = tk
            for k in range(2):
                sb4 = (ip % 2) * 2 + k
                b.wait("scalar", tk)
                tsl = b.op("scalar", (lambda sb4, bbase, k, ip: lambda e: e.activation(
                    silu_sb[sb4], ps[bbase + k], AF.Silu, scale=gs_sb[:, ip:ip + 1]))(sb4, bbase, k, ip), s_act, 1)
                bankfree[bbase + k] = tsl
                b.wait("vector", tsl)
                th = b.op("vector", (lambda sb4, bbase, k, ip: lambda e: e.scalar_tensor_tensor(
                    hdn[:, ip * 1024 + k * 512: ip * 1024 + (k + 1) * 512],
                    ps[bbase + 2 + k], us_sb[:, ip:ip + 1], silu_sb[sb4], ALU.mult, ALU.mult))(sb4, bbase, k, ip), s_dve, 1)
                bankfree[bbase + 2 + k] = th
                hdn_last = th
        tok_hdn = hdn_last
        ln2fm_free = gu_last
        if DBG and p == 0:
            b.wait("sync", tok_hdn)
            b.op("sync", lambda e: e.dma_start(hdn_dbg, hdn), m_dbg, 16)
        down_last = None
        for o in range(4):
            for ip in range(NIC):
                wb = ip % 4
                b.wait("sync", tmax([dwp_free[wb]], tok_dwprep if p == 0 and o == 0 and ip < 4 else None))
                td = b.op("sync", (lambda ip, o, wb: lambda e: e.dma_start(
                    dwp[wb], dwbf_sc[ip][:, o * 512:(o + 1) * 512]))(ip, o, wb), m_dwp[wb], 16)
                if ip == 0:
                    b.wait("tensor", tmax([tok_hdn], *[[bankfree[i]] for i in range(8)]))
                b.wait("tensor", td)
                for j in range(8):
                    last = (j == 7)
                    tk = b.op("tensor", (lambda wb, ip, j: lambda e: e.matmul(
                        ps[j], hdn[:, ip * 1024 + j * 128: ip * 1024 + (j + 1) * 128],
                        dwp[wb], start=(ip == 0), stop=(ip == NIC - 1)))(wb, ip, j),
                        s_pe if last else None, 1 if last else 0)
                dwp_free[wb] = tk
            down_last = tk
            for j in range(8):
                fb = j % 4
                b.wait("vector", tmax([down_last], [t_ds], [ffnst_free[fb]]))
                tvd = b.op("vector", (lambda j, o, fb: lambda e: e.tensor_tensor(
                    ffnst[fb], ps[j], dsbc_sb[:, o * 512:(o + 1) * 512], ALU.mult))(j, o, fb), s_dve, 1)
                bankfree[j] = tvd
                b.wait("gpsimd", tvd)
                tj = p * 8 + j
                tsf = b.op("gpsimd", (lambda tj, o, fb: lambda e: e.dma_start(
                    ffn_sc[tj * 128:(tj + 1) * 128, o * 512:(o + 1) * 512], ffnst[fb]))(tj, o, fb), m_ffn[fb], 16)
                ffnst_free[fb] = tsf
            if p > 0:
                emit_epilogue_tile(p - 1, 2 * o)
                emit_epilogue_tile(p - 1, 2 * o + 1)
        hdn_free = down_last
        ffn_done_tok[0] = tmax([b.final(s) for s in m_ffn])

    for j in range(8):
        emit_epilogue_tile(3, j)
    tok_p4_sync = tmax([b.final(s) for s in m_ffn])

    return _finish(nc, b)

def _finish(nc, b):
    for name in b.semnames:
        if b.own[name] is None and b.counts[name] > 0:
            b.q["sync"].append(("w", name, b.counts[name]))
    stack = contextlib.ExitStack()
    b.emit(nc, stack)
    stack.close()
    return nc


def stage_inputs(L0_attn_Wqkv, L0_attn_bqkv, L0_attn_Wo, L0_attn_bo,
                 L0_ffn_gate_w, L0_ffn_up_w, L0_ffn_down_w,
                 L0_ffn_gate_s, L0_ffn_up_s, L0_ffn_down_s):
    f32 = lambda a: np.ascontiguousarray(np.asarray(a, dtype=np.float32))
    Wqkv = f32(L0_attn_Wqkv)
    sh = {}
    sh["wqkv_sw"] = np.ascontiguousarray(Wqkv.reshape(NDC, 128, 12, 512).transpose(2, 1, 0, 3))
    bq = f32(L0_attn_bqkv)
    sh["bqkv_t"] = np.ascontiguousarray(bq.reshape(48, 128).T)
    sh["bv_bc"] = np.ascontiguousarray(np.broadcast_to(bq[4096:6144], (128, D)))
    sh["wo_n"] = f32(L0_attn_Wo)
    sh["bo_bc"] = np.ascontiguousarray(np.broadcast_to(f32(L0_attn_bo), (128, D)))
    sh["gw_sw"] = np.ascontiguousarray(f32(L0_ffn_gate_w).reshape(NIC, 128, NDC, 128).transpose(0, 3, 2, 1))
    sh["uw_sw"] = np.ascontiguousarray(f32(L0_ffn_up_w).reshape(NIC, 128, NDC, 128).transpose(0, 3, 2, 1))
    sh["dw_t"] = np.ascontiguousarray(f32(L0_ffn_down_w).T.reshape(NIC, 128, D))
    sh["gs_t"] = np.ascontiguousarray(f32(L0_ffn_gate_s).reshape(NIC, 128).T)
    sh["us_t"] = np.ascontiguousarray(f32(L0_ffn_up_s).reshape(NIC, 128).T)
    sh["ds_bc"] = np.ascontiguousarray(np.broadcast_to(f32(L0_ffn_down_s), (128, D)))
    sl, tl = np.arange(128)[:, None], np.arange(512)[None, :]
    masks = np.stack([(tl >= sl + 128 * k) for k in range(4)], 0).astype(ml_dtypes.bfloat16)
    sh["masks"] = np.ascontiguousarray(masks.transpose(1, 0, 2).reshape(128, 2048))
    sh["ident_f32"] = np.eye(128, dtype=np.float32)
    sh["ones_f32"] = np.ones((1, 128), dtype=np.float32)
    return sh


def kernel(L0_attn_x, L0_attn_Wqkv, L0_attn_bqkv, L0_attn_Wo, L0_attn_bo,
           L0_ffn_gate_w, L0_ffn_up_w, L0_ffn_down_w,
           L0_ffn_gate_s, L0_ffn_up_s, L0_ffn_down_s):
    x = np.ascontiguousarray(np.asarray(L0_attn_x, dtype=np.float32))
    sh = stage_inputs(L0_attn_Wqkv, L0_attn_bqkv, L0_attn_Wo, L0_attn_bo,
                      L0_ffn_gate_w, L0_ffn_up_w, L0_ffn_down_w,
                      L0_ffn_gate_s, L0_ffn_up_s, L0_ffn_down_s)
    nc = build_nc()
    in_maps = [dict(sh, x=np.ascontiguousarray(x[c])) for c in range(8)]
    res = run_bass_kernel_spmd(nc, in_maps, core_ids=list(range(8)))
    return np.stack([r["out"] for r in res.results], axis=0)


# version 16
# speedup vs baseline: 1.0205x; 1.0205x over previous
import contextlib
import numpy as np
import ml_dtypes
import concourse.bass as bass
import concourse.mybir as mybir
from concourse.bass_utils import run_bass_kernel_spmd

F32 = mybir.dt.float32
BF16 = mybir.dt.bfloat16
AF = mybir.ActivationFunctionType
AX = mybir.AxisListType
ALU = mybir.AluOpType

S = 4096
D = 2048
NDC = 16
NIC = 44
NH = 16
SCALE = np.float32(1.0 / np.sqrt(128.0))
ESHIFT = np.float32(-8.0)
EPS = 1e-5

ENGS = ("sync", "gpsimd", "tensor", "scalar", "vector")
PHASES = 5
DBG = False


def tmax(*toks):
    out = {}

    def add(t):
        if t is None:
            return
        if isinstance(t, list):
            for u in t:
                add(u)
        else:
            s, v = t
            if v > out.get(s, -1):
                out[s] = v
    for t in toks:
        add(t)
    return [(s, v) for s, v in out.items()] or None


class Builder:
    def __init__(self):
        self.q = {e: [] for e in ENGS}
        self.semnames = []
        self.counts = {}
        self.waited = {e: {} for e in ENGS}
        self.own = {}

    def sem(self, name, owner):
        self.semnames.append(name)
        self.counts[name] = 0
        self.own[name] = owner
        return name

    def wait(self, eng, tok):
        if tok is None:
            return
        if isinstance(tok, list):
            for t in tmax(tok):
                self._wait1(eng, t)
            return
        self._wait1(eng, tok)

    def _wait1(self, eng, tok):
        s, v = tok
        if v <= 0:
            return
        if self.own[s] == eng:
            return
        if self.waited[eng].get(s, -1) >= v:
            return
        self.waited[eng][s] = v
        self.q[eng].append(("w", s, v))

    def wait_own(self, eng, tok):
        s, v = tok
        if v <= 0 or self.waited[eng].get(s, -1) >= v:
            return
        self.waited[eng][s] = v
        self.q[eng].append(("w", s, v))

    def op(self, eng, fn, sem=None, inc=0):
        self.q[eng].append(("o", fn, sem, inc))
        if sem is not None:
            self.counts[sem] += inc
            return (sem, self.counts[sem])
        return None

    def final(self, sem):
        return (sem, self.counts[sem])

    def emit(self, nc, stack):
        hs = {n: stack.enter_context(nc.semaphore(n)) for n in self.semnames}
        with nc.Block() as block:
            for eng in ENGS:
                items = self.q[eng]

                def make(items):
                    def body(e):
                        for it in items:
                            if it[0] == "w":
                                e.wait_ge(hs[it[1]], it[2])
                            else:
                                ins = it[1](e)
                                if it[2] is not None:
                                    ins.then_inc(hs[it[2]], it[3])
                    return body

                getattr(block, eng)(make(items))


def build_nc():
    nc = bass.Bass("TRN2", target_bir_lowering=False)
    b = Builder()
    s_pe = b.sem("s_pe", "tensor")
    s_act = b.sem("s_act", "scalar")
    s_dve = b.sem("s_dve", "vector")
    s_gp = b.sem("s_gp", "gpsimd")
    ds = lambda name, n: [b.sem("%s%d" % (name, i), None) for i in range(n)]
    m_const = b.sem("m_const", None)
    m_x = ds("m_x", 2)
    m_wq = ds("m_wq", 2)
    m_qkst = ds("m_qkst", 2)
    m_vst = ds("m_vst", 2)
    m_head = ds("m_head", 2)
    m_af = ds("m_af", 2)
    m_dwl = ds("m_dwl", 2)
    m_dws = ds("m_dws", 2)
    m_afm = b.sem("m_afm", None)
    m_wo = b.sem("m_wo", None)
    m_y = ds("m_y", 2)
    m_l2st = b.sem("m_l2st", None)
    m_l2f = b.sem("m_l2f", None)
    m_gw = ds("m_gw", 2)
    m_uw = ds("m_uw", 2)
    m_dwp = ds("m_dwp", 4)
    m_ffn = ds("m_ffn", 4)
    m_ft = ds("m_ft", 2)
    m_y5 = ds("m_y5", 2)
    m_out = ds("m_out", 2)
    m_t1 = b.sem("m_t1", None)
    m_t2 = b.sem("m_t2", None)
    m_dbg = b.sem("m_dbg", None)

    x_h = nc.dram_tensor("x", [S, D], F32, kind="ExternalInput").ap()
    wqkv_h = nc.dram_tensor("wqkv_sw", [12, 128, NDC, 512], F32, kind="ExternalInput").ap()
    bqkv_h = nc.dram_tensor("bqkv_t", [128, 48], F32, kind="ExternalInput").ap()
    bvbc_h = nc.dram_tensor("bv_bc", [128, D], F32, kind="ExternalInput").ap()
    wo_h = nc.dram_tensor("wo_n", [D, D], F32, kind="ExternalInput").ap()
    bobc_h = nc.dram_tensor("bo_bc", [128, D], F32, kind="ExternalInput").ap()
    gw_h = nc.dram_tensor("gw_sw", [NIC, 128, NDC, 128], F32, kind="ExternalInput").ap()
    uw_h = nc.dram_tensor("uw_sw", [NIC, 128, NDC, 128], F32, kind="ExternalInput").ap()
    dw_h = nc.dram_tensor("dw_t", [NIC, 128, D], F32, kind="ExternalInput").ap()
    gs_h = nc.dram_tensor("gs_t", [128, NIC], F32, kind="ExternalInput").ap()
    us_h = nc.dram_tensor("us_t", [128, NIC], F32, kind="ExternalInput").ap()
    dsbc_h = nc.dram_tensor("ds_bc", [128, D], F32, kind="ExternalInput").ap()
    mask_h = nc.dram_tensor("masks", [128, 2048], BF16, kind="ExternalInput").ap()
    idf_h = nc.dram_tensor("ident_f32", [128, 128], F32, kind="ExternalInput").ap()
    onef_h = nc.dram_tensor("ones_f32", [1, 128], F32, kind="ExternalInput").ap()
    out_h = nc.dram_tensor("out", [S, D], F32, kind="ExternalOutput").ap()

    KI = "ExternalOutput" if DBG else "Internal"
    q_sc = nc.dram_tensor("q_sc", [NH, 128, S], BF16, kind=KI).ap()
    k_sc = nc.dram_tensor("k_sc", [NH, 128, S], BF16, kind=KI).ap()
    v_sc = nc.dram_tensor("v_sc", [NH, 32, 128, 128], BF16, kind=KI).ap()
    af_sc = nc.dram_tensor("af_sc", [NDC, 128, S], BF16, kind=KI).ap()
    l2_sc = nc.dram_tensor("l2_sc", [NDC, 128, S], BF16, kind=KI).ap()
    dwbf_sc = nc.dram_tensor("dwbf_sc", [NIC, 128, D], BF16, kind=KI).ap()
    ffn_sc = nc.dram_tensor("ffn_sc", [S, D], F32, kind=KI).ap()
    hdn_dbg = nc.dram_tensor("hdn_dbg", [128, 45056], BF16, kind=KI).ap() if DBG else None

    pbf = nc.alloc_sbuf_tensor("pool_bf", [128, 71680], BF16).ap()
    pf = nc.alloc_sbuf_tensor("pool_f32", [128, 14336], F32).ap()
    mask_sb = nc.alloc_sbuf_tensor("mask_sb", [128, 2048], BF16).ap()
    idf_sb = nc.alloc_sbuf_tensor("idf_sb", [128, 128], F32).ap()
    onef_sb = nc.alloc_sbuf_tensor("onef_sb", [1, 128], F32).ap()
    bqkv_sb = nc.alloc_sbuf_tensor("bqkv_sb", [128, 48], F32).ap()
    gs_sb = nc.alloc_sbuf_tensor("gs_sb", [128, NIC], F32).ap()
    us_sb = nc.alloc_sbuf_tensor("us_sb", [128, NIC], F32).ap()
    st_sb = nc.alloc_sbuf_tensor("st_sb", [128, 16], F32).ap()
    iz_sb = nc.alloc_sbuf_tensor("iz_sb", [1, 1024], F32).ap()
    zbc_sb = nc.alloc_sbuf_tensor("zbc_sb", [128, 512], F32).ap()

    ps = [nc.alloc_psum_tensor("ps%d" % i, [128, 512], F32).ap() for i in range(8)]
    bankfree = [None] * 8

    b.op("sync", lambda e: e.dma_start(mask_sb[:], mask_h), m_const, 16)
    b.op("sync", lambda e: e.dma_start(idf_sb[:], idf_h), m_const, 16)
    b.op("sync", lambda e: e.dma_start(onef_sb[:], onef_h), m_const, 16)
    b.op("sync", lambda e: e.dma_start(bqkv_sb[:], bqkv_h), m_const, 16)
    b.op("sync", lambda e: e.dma_start(gs_sb[:], gs_h), m_const, 16)
    tok_const = b.op("sync", lambda e: e.dma_start(us_sb[:], us_h), m_const, 16)

    def emit_ln(c, tile, sq_off):
        b.op("vector", (lambda c, tile: lambda e: e.reduce_sum(st_sb[:, c:c + 1], tile, axis=AX.X))(c, tile), s_dve, 1)
        t2 = b.op("vector", (lambda c, tile, sq_off: lambda e: e.scalar_tensor_tensor(
            pf[:, sq_off:sq_off + D], tile, 1.0, tile, ALU.mult, ALU.mult,
            accum_out=st_sb[:, c + 1:c + 2]))(c, tile, sq_off), s_dve, 1)
        b.wait_own("vector", t2)
        b.op("vector", (lambda c: lambda e: e.tensor_scalar(st_sb[:, c:c + 1], st_sb[:, c:c + 1], 1.0 / D, 0.0, ALU.mult, ALU.add))(c), s_dve, 1)
        t4 = b.op("vector", (lambda c: lambda e: e.tensor_scalar(st_sb[:, c + 1:c + 2], st_sb[:, c + 1:c + 2], 1.0 / D, EPS, ALU.mult, ALU.add))(c), s_dve, 1)
        b.wait_own("vector", t4)
        t5 = b.op("vector", (lambda c: lambda e: e.tensor_tensor(st_sb[:, c + 2:c + 3], st_sb[:, c:c + 1], st_sb[:, c:c + 1], ALU.mult))(c), s_dve, 1)
        b.op("vector", (lambda c, tile: lambda e: e.tensor_scalar(tile, tile, st_sb[:, c:c + 1], 0.0, ALU.subtract, ALU.add))(c, tile))
        b.wait_own("vector", t5)
        t7 = b.op("vector", (lambda c: lambda e: e.tensor_tensor(st_sb[:, c + 1:c + 2], st_sb[:, c + 1:c + 2], st_sb[:, c + 2:c + 3], ALU.subtract))(c), s_dve, 1)
        b.wait("scalar", t7)
        t8 = b.op("scalar", (lambda c: lambda e: e.activation(st_sb[:, c + 2:c + 3], st_sb[:, c + 1:c + 2], AF.Ln, bias=0.0, scale=np.float32(1.0)))(c), s_act, 1)
        b.wait_own("scalar", t8)
        ta = b.op("scalar", (lambda c: lambda e: e.activation(st_sb[:, c + 3:c + 4], st_sb[:, c + 2:c + 3], AF.Exp, bias=0.0, scale=np.float32(-0.5)))(c), s_act, 1)
        return ta

    ln1fm = [pbf[:, 0:16384], pbf[:, 16384:32768]]
    wqp = [pbf[:, 32768:40960], pbf[:, 40960:49152]]
    qkst = [pbf[:, 49152:51200], pbf[:, 51200:53248]]
    vst = [pbf[:, 53248:53760], pbf[:, 53760:54272]]
    xt = [pf[:, 0:2048], pf[:, 2048:4096]]
    ln1f = pf[:, 4096:6144]
    bvbc_sb = pf[:, 8192:10240]

    t_bv = b.op("sync", lambda e: e.dma_start(bvbc_sb, bvbc_h), m_const, 16)
    b.wait("tensor", t_bv)
    b.wait("vector", t_bv)
    b.wait("scalar", t_bv)

    xt_free = [None, None]
    wqp_free = [None, None]
    qkst_free = [None, None]
    vst_free = [None, None]
    ln1fm_ready = [[None, None], [None, None]]
    ln1fm_free = [None, None]
    ln1f_free = [None]

    def emit_ln_tile(p, j):
        par = p % 2
        tj = p * 8 + j
        xb = j % 2
        b.wait("sync", xt_free[xb])
        tl = b.op("sync", (lambda tj, xb: lambda e: e.dma_start(xt[xb], x_h[tj * 128:(tj + 1) * 128, :]))(tj, xb), m_x[xb], 16)
        b.wait("vector", tmax([tl]))
        ta = emit_ln(0, xt[xb], 6144)
        b.wait("vector", tmax([ta], [ln1f_free[0]]))
        tv = b.op("vector", (lambda xb: lambda e: e.tensor_scalar(ln1f, xt[xb], st_sb[:, 3:4], 0.0, ALU.mult, ALU.add))(xb), s_dve, 1)
        xt_free[xb] = tv
        last_evac = None
        for qd in range(4):
            bk = 4 + qd
            b.wait("tensor", tmax([tv], [bankfree[bk]], [ln1fm_free[par]] if j == 0 and qd == 0 else None))
            for w in range(4):
                dc = qd * 4 + w
                tk = b.op("tensor", (lambda bk, w, dc: lambda e: e.transpose(
                    ps[bk][:, w * 128:(w + 1) * 128], ln1f[:, dc * 128:(dc + 1) * 128], idf_sb))(bk, w, dc),
                    s_pe if w == 3 else None, 1 if w == 3 else 0)
            ln1f_free[0] = tk
            b.wait("vector", tmax([tk]))
            dst = ln1fm[par].rearrange("p (dc t) -> p dc t", t=1024)[:, qd * 4:(qd + 1) * 4, j * 128:(j + 1) * 128]
            srcv = ps[bk].rearrange("p (w t) -> p w t", t=128)
            te = b.op("vector", (lambda dst, srcv: lambda e: e.tensor_copy(dst, srcv))(dst, srcv), s_dve, 1)
            bankfree[bk] = te
            last_evac = te
        if j == 3:
            ln1fm_ready[par][0] = last_evac
        if j == 7:
            ln1fm_ready[par][1] = last_evac

    for j in range(8):
        emit_ln_tile(0, j)

    for p in range(4):
        par = p % 2
        mmtok = None
        for pn in range(12):
            wb = pn % 2
            b.wait("gpsimd", tmax([wqp_free[wb]], [tok_const] if p == 0 and pn < 2 else None))
            tw = b.op("gpsimd", (lambda pn, wb: lambda e: e.dma_start(
                wqp[wb].rearrange("p (dc o) -> p dc o", o=512), wqkv_h[pn]))(pn, wb), m_wq[wb], 16)
            if pn < 8:
                for k in range(2):
                    sb = (pn * 2 + k) % 2
                    evtoks = []
                    for oc in range(4):
                        bk = (pn * 8 + k * 4 + oc) % 4
                        b.wait("tensor", tmax([tw], [ln1fm_ready[par][k]], [bankfree[bk]]))
                        for dc in range(NDC):
                            tk = b.op("tensor", (lambda wb, dc, oc, bk, par, k: lambda e: e.matmul(
                                ps[bk], wqp[wb][:, dc * 512 + oc * 128: dc * 512 + (oc + 1) * 128],
                                ln1fm[par][:, dc * 1024 + k * 512: dc * 1024 + (k + 1) * 512],
                                start=(dc == 0), stop=(dc == NDC - 1)))(wb, dc, oc, bk, par, k),
                                s_pe if dc == NDC - 1 else None, 1 if dc == NDC - 1 else 0)
                        mmtok = tk
                        cglob = pn * 4 + oc
                        b.wait("scalar", tmax([tk], [qkst_free[sb]] if oc == 0 else None))
                        tac = b.op("scalar", (lambda sb, oc, bk, cglob: lambda e: e.activation(
                            qkst[sb][:, oc * 512:(oc + 1) * 512], ps[bk], AF.Identity,
                            bias=bqkv_sb[:, cglob:cglob + 1]))(sb, oc, bk, cglob), s_act, 1)
                        bankfree[bk] = tac
                        evtoks.append(tac)
                    b.wait("sync", evtoks)
                    dest = q_sc if pn < 4 else k_sc
                    h0 = (pn * 4) % 16
                    t0 = p * 1024 + k * 512
                    tst = b.op("sync", (lambda dest, h0, t0, sb: lambda e: e.dma_start(
                        dest[h0:h0 + 4, :, t0:t0 + 512].rearrange("h p t -> p h t"),
                        qkst[sb].rearrange("p (h t) -> p h t", h=4)))(dest, h0, t0, sb), m_qkst[sb], 16)
                    qkst_free[sb] = tst
            else:
                for j in range(8):
                    tj = p * 8 + j
                    sb = (pn * 8 + j) % 2
                    bk = (pn * 8 + j) % 4
                    b.wait("tensor", tmax([tw], [ln1fm_ready[par][j // 4]], [bankfree[bk]]))
                    for dc in range(NDC):
                        tk = b.op("tensor", (lambda wb, dc, j, bk, par: lambda e: e.matmul(
                            ps[bk], ln1fm[par][:, dc * 1024 + j * 128: dc * 1024 + (j + 1) * 128],
                            wqp[wb][:, dc * 512:(dc + 1) * 512],
                            start=(dc == 0), stop=(dc == NDC - 1)))(wb, dc, j, bk, par),
                            s_pe if dc == NDC - 1 else None, 1 if dc == NDC - 1 else 0)
                    mmtok = tk
                    b.wait("vector", tmax([tk], [vst_free[sb]], [t_bv]))
                    tvv = b.op("vector", (lambda sb, bk, pn: lambda e: e.scalar_tensor_tensor(
                        vst[sb], ps[bk], 1.0, bvbc_sb[:, (pn - 8) * 512:(pn - 7) * 512],
                        ALU.mult, ALU.add))(sb, bk, pn), s_dve, 1)
                    bankfree[bk] = tvv
                    b.wait("sync", tvv)
                    h0 = (pn - 8) * 4
                    tsv = b.op("sync", (lambda sb, h0, tj: lambda e: e.dma_start(
                        v_sc[h0:h0 + 4, tj, :, :].rearrange("h p w -> p h w"),
                        vst[sb].rearrange("p (h w) -> p h w", w=128)))(sb, h0, tj), m_vst[sb], 16)
                    vst_free[sb] = tsv
            wqp_free[wb] = mmtok
            if p < 3 and 4 <= pn < 12:
                emit_ln_tile(p + 1, pn - 4)
        ln1fm_free[par] = mmtok

    tok_p1_sync = tmax([b.final(s) for s in m_qkst + m_vst])
    tok_p1_pe = (s_pe, b.counts["s_pe"])

    if PHASES < 2:
        return _finish(nc, b)

    qb = [pbf[:, 0:4096], pbf[:, 4096:8192]]
    kb = [pbf[:, 8192:12288], pbf[:, 12288:16384]]
    pbuf = [pbf[:, 16384:32768], pbf[:, 46336:62720]]
    va_view = [pbf[:, 32768 + k * 4224: 32768 + (k + 1) * 4224].rearrange("p (c w) -> p c w", w=132) for k in range(2)]
    afst = [pbf[:, 41216:41728], pbf[:, 41728:42240]]
    dwc = [pbf[:, 42240:44288], pbf[:, 44288:46336]]

    b.wait("vector", tok_p1_pe)
    tok_ones = []
    for k2 in range(2):
        tok_ones.append(b.op("vector", (lambda k2: lambda e: e.memset(va_view[k2][:, :, 128:129], 1.0))(k2), s_dve, 1))

    pe_head_last = [tok_p1_pe, tok_p1_pe]
    afst_free = [tok_p1_pe, tok_p1_pe]
    pbuf_free = [None, None]
    iz_free = [None, None]
    dwc_store = [None, None]
    dw_ip = [0]
    head_loads = [None, None]

    def emit_head_loads(h):
        hb = h % 2
        b.wait("sync", tmax(tok_p1_sync, [pe_head_last[hb]], tok_ones if h < 2 else None))
        b.op("sync", (lambda h, hb: lambda e: e.dma_start(qb[hb], q_sc[h]))(h, hb), m_head[hb], 16)
        b.op("sync", (lambda h, hb: lambda e: e.dma_start(kb[hb], k_sc[h]))(h, hb), m_head[hb], 16)
        ld = None
        for vq in range(4):
            ld = b.op("sync", (lambda h, hb, vq: lambda e: e.dma_start(
                va_view[hb][:, vq * 8:(vq + 1) * 8, 0:128],
                v_sc[h][vq * 8:(vq + 1) * 8].rearrange("c p w -> p c w")))(h, hb, vq), m_head[hb], 16)
        head_loads[hb] = ld
        for _ in range(3):
            ip = dw_ip[0]
            if ip < NIC:
                cb = ip % 2
                b.wait("gpsimd", tmax([tok_p1_pe] if ip < 2 else None, [dwc_store[cb]]))
                tld = b.op("gpsimd", (lambda ip, cb: lambda e: e.dma_start(dwc[cb], dw_h[ip]))(ip, cb), m_dwl[cb], 16)
                b.wait("sync", tld)
                dwc_store[cb] = b.op("sync", (lambda ip, cb: lambda e: e.dma_start(dwbf_sc[ip], dwc[cb]))(ip, cb), m_dws[cb], 16)
                dw_ip[0] += 1

    def emit_pv_chunk(u, c):
        hb, par, nk = u["hb"], u["par"], u["nk"]
        bo_, bz = 2 + 3 * par, 3 + 3 * par
        b.wait("tensor", tmax([u["exp"][c]],
                              [bankfree[bo_], bankfree[bz]] if c == 0 else None))
        pb = pbuf[par]
        b.op("tensor", (lambda hb, c, st, sp, bo_, pb: lambda e: e.matmul(
            ps[bo_], va_view[hb][:, c, 0:128], pb[:, c * 512:(c + 1) * 512],
            start=st, stop=sp))(hb, c, c == 0, c == nk - 1, bo_, pb))
        tz = b.op("tensor", (lambda hb, c, st, sp, bz, pb: lambda e: e.matmul(
            ps[bz][0:1, :], va_view[hb][:, c, 128:129], pb[:, c * 512:(c + 1) * 512],
            start=st, stop=sp))(hb, c, c == 0, c == nk - 1, bz, pb), s_pe, 1)
        return tz

    def emit_recip(u, pv_last):
        par = u["par"]
        bz = 3 + 3 * par
        b.wait("vector", tmax([pv_last], [iz_free[par]]))
        trc = b.op("vector", (lambda bz, par: lambda e: e.reciprocal(
            iz_sb[0:1, par * 512:(par + 1) * 512], ps[bz][0:1, :]))(bz, par), s_dve, 1)
        bankfree[bz] = trc
        u["trc"] = trc

    def emit_evacs(u):
        par = u["par"]
        bo_, bb_ = 2 + 3 * par, 4 + 3 * par
        b.wait("tensor", tmax([u["trc"]], [bankfree[bb_]]))
        tbc = b.op("tensor", (lambda bb_, par: lambda e: e.matmul(
            ps[bb_], onef_sb, iz_sb[0:1, par * 512:(par + 1) * 512], start=True, stop=True))(bb_, par), s_pe, 1)
        iz_free[par] = tbc
        b.wait("vector", tbc)
        tcp = b.op("vector", (lambda bb_: lambda e: e.tensor_copy(zbc_sb, ps[bb_]))(bb_), s_dve, 1)
        bankfree[bb_] = tcp
        ab = u["ab"]
        b.wait("vector", afst_free[ab])
        tstt = b.op("vector", (lambda bo_, ab: lambda e: e.scalar_tensor_tensor(
            afst[ab], ps[bo_], 1.0, zbc_sb, ALU.mult, ALU.mult))(bo_, ab), s_dve, 1)
        bankfree[bo_] = tstt
        b.wait("sync", tstt)
        tsa = b.op("sync", (lambda u, ab: lambda e: e.dma_start(
            af_sc[u["h"], :, u["tq"] * 512:(u["tq"] + 1) * 512], afst[ab]))(u, ab), m_af[ab], 16)
        afst_free[ab] = tsa

    prev = None
    pend_evac = []
    for i in range(NH * 8):
        h, tq = i // 8, i % 8
        if i == 0:
            emit_head_loads(0)
        if tq == 1 and h + 1 < NH:
            emit_head_loads(h + 1)
        hb = h % 2
        par = i % 2
        nk = 4 * (tq + 1)
        u = {"h": h, "tq": tq, "hb": hb, "par": par, "nk": nk, "exp": {}, "ab": i % 2}
        if len(pend_evac) == 1:
            emit_evacs(pend_evac.pop(0))
        prev_nk = prev["nk"] if prev else 0
        pv_last = None
        for c in range(max(nk, prev_nk)):
            if c < nk:
                bk = c % 2
                b.wait("tensor", tmax([head_loads[hb]], [bankfree[bk]]))
                tsc = b.op("tensor", (lambda hb, c, tq, bk: lambda e: e.matmul(
                    ps[bk], kb[hb][:, c * 128:(c + 1) * 128], qb[hb][:, tq * 512:(tq + 1) * 512],
                    start=True, stop=True))(hb, c, tq, bk), s_pe, 1)
                b.wait("scalar", tmax([tsc], [pbuf_free[par]] if c == 0 else None))
                te = b.op("scalar", (lambda c, bk, par: lambda e: e.activation(
                    pbuf[par][:, c * 512:(c + 1) * 512], ps[bk], AF.Exp, bias=ESHIFT, scale=SCALE))(c, bk, par), s_act, 1)
                bankfree[bk] = te
                if c >= 4 * tq:
                    kk = c - 4 * tq
                    b.wait("gpsimd", te)
                    te = b.op("gpsimd", (lambda c, kk, par: lambda e: e.tensor_tensor(
                        pbuf[par][:, c * 512:(c + 1) * 512], pbuf[par][:, c * 512:(c + 1) * 512],
                        mask_sb[:, kk * 512:(kk + 1) * 512], ALU.mult))(c, kk, par), s_gp, 1)
                u["exp"][c] = te
            if prev is not None and c < prev_nk:
                pv_last = emit_pv_chunk(prev, c)
        if prev is not None:
            pbuf_free[prev["par"]] = pv_last
            if prev["tq"] == 7:
                pe_head_last[prev["hb"]] = pv_last
            emit_recip(prev, pv_last)
            pend_evac.append(prev)
        prev = u
    pv_last = None
    for c in range(prev["nk"]):
        pv_last = emit_pv_chunk(prev, c)
    pbuf_free[prev["par"]] = pv_last
    pe_head_last[prev["hb"]] = pv_last
    emit_recip(prev, pv_last)
    pend_evac.append(prev)
    for uu in pend_evac:
        emit_evacs(uu)

    tok_p2_sync = tmax([b.final(s) for s in m_af])
    tok_p2_pe = (s_pe, b.counts["s_pe"])
    tok_p2_act = (s_act, b.counts["s_act"])
    tok_p2_dve = (s_dve, b.counts["s_dve"])
    tok_dwprep = tmax(dwc_store)

    if PHASES < 3:
        return _finish(nc, b)

    afm = pbf[:, 0:16384]
    wo_sb = pbf[:, 16384:49152]
    l2st = pbf[:, 49152:57344]
    projs = [pf[:, 4096:6144], pf[:, 10240:12288]]
    yt = [pf[:, 6144:8192], pf[:, 8192:10240]]
    sqo = [6144, 8192]
    bobc_sb = pf[:, 12288:14336]
    yt_free = [tok_p2_dve, tok_p2_dve]

    twl = None
    for dc in range(NDC):
        b.wait("gpsimd", tmax([tok_p2_pe], [tok_p2_act], [tok_p2_dve], tok_dwprep))
        twl = b.op("gpsimd", (lambda dc: lambda e: e.dma_start(
            wo_sb[:, dc * 2048:(dc + 1) * 2048], wo_h[dc * 128:(dc + 1) * 128, :]))(dc), m_wo, 16)
    tok_wo = twl
    b.wait("sync", tmax([tok_p2_pe], [tok_p2_dve]))
    t_bo = b.op("sync", lambda e: e.dma_start(bobc_sb, bobc_h), m_const, 16)

    afm_free = tok_p2_pe
    l2st_free = None
    xt_free3 = [xt_free[0], xt_free[1]]
    proj_free = [None, None]
    l2_state = {"free": None}

    def emit_l2_transp(jj, p, tv2_tok):
        last = None
        proj = projs[jj % 2]
        for qd in range(4):
            bk = 4 + qd
            b.wait("tensor", tmax([tv2_tok], [bankfree[bk]]))
            for w in range(4):
                dc = qd * 4 + w
                tk2 = b.op("tensor", (lambda bk, w, dc, proj: lambda e: e.transpose(
                    ps[bk][:, w * 128:(w + 1) * 128], proj[:, dc * 128:(dc + 1) * 128], idf_sb))(bk, w, dc, proj),
                    s_pe if w == 3 else None, 1 if w == 3 else 0)
            last = tk2
            b.wait("vector", tmax([tk2], [l2_state["free"]] if qd == 0 and jj % 4 == 0 else None))
            dst = l2st.rearrange("p (hc t) -> p hc t", t=512)[:, qd * 4:(qd + 1) * 4, (jj % 4) * 128:((jj % 4) + 1) * 128]
            srcv = ps[bk].rearrange("p (w t) -> p w t", t=128)
            te2 = b.op("vector", (lambda dst, srcv: lambda e: e.tensor_copy(dst, srcv))(dst, srcv), s_dve, 1)
            bankfree[bk] = te2
        if jj % 4 == 3:
            b.wait("sync", te2)
            t0 = p * 1024 + (jj // 4) * 512
            fl = None
            for hc in range(NDC):
                fl = b.op("sync", (lambda hc, t0: lambda e: e.dma_start(
                    l2_sc[hc][:, t0:t0 + 512], l2st[:, hc * 512:(hc + 1) * 512]))(hc, t0), m_l2st, 16)
            l2_state["free"] = fl
        return last

    for p in range(4):
        b.wait("sync", tmax([afm_free], tok_p2_sync))
        tok_afm = None
        for hc in range(NDC):
            tok_afm = b.op("sync", (lambda hc, p: lambda e: e.dma_start(
                afm[:, hc * 1024:(hc + 1) * 1024], af_sc[hc][:, p * 1024:(p + 1) * 1024]))(hc, p), m_afm, 16)
        mm_last = None
        pend = None
        for j in range(8):
            tj = p * 8 + j
            xb = j % 2
            proj = projs[j % 2]
            b.wait("sync", xt_free3[xb])
            tlx = b.op("sync", (lambda tj, xb: lambda e: e.dma_start(xt[xb], x_h[tj * 128:(tj + 1) * 128, :]))(tj, xb), m_x[xb], 16)
            for o in range(4):
                bk = (j * 4 + o) % 4
                b.wait("tensor", tmax([tok_afm], [tok_wo], [bankfree[bk]]))
                for dc in range(NDC):
                    tk = b.op("tensor", (lambda dc, j, o, bk: lambda e: e.matmul(
                        ps[bk], afm[:, dc * 1024 + j * 128: dc * 1024 + (j + 1) * 128],
                        wo_sb[:, dc * 2048 + o * 512: dc * 2048 + (o + 1) * 512],
                        start=(dc == 0), stop=(dc == NDC - 1)))(dc, j, o, bk),
                        s_pe if dc == NDC - 1 else None, 1 if dc == NDC - 1 else 0)
                mm_last = tk
                b.wait("vector", tmax([tk], [t_bo], [proj_free[j % 2]] if o == 0 else None,
                                      [yt_free[1 - xb]] if o == 0 else None))
                tvp = b.op("vector", (lambda o, bk, proj: lambda e: e.scalar_tensor_tensor(
                    proj[:, o * 512:(o + 1) * 512], ps[bk], 1.0, bobc_sb[:, o * 512:(o + 1) * 512],
                    ALU.mult, ALU.add, accum_out=st_sb[:, 12 + o:13 + o]))(o, bk, proj), s_dve, 1)
                bankfree[bk] = tvp
            sq_t = pf[:, sqo[1 - xb]:sqo[1 - xb] + D]
            b.wait_own("vector", tvp)
            t_c1 = b.op("vector", lambda e: e.tensor_tensor(st_sb[:, 12:13], st_sb[:, 12:13], st_sb[:, 13:14], ALU.add), s_dve, 1)
            t_c2 = b.op("vector", lambda e: e.tensor_tensor(st_sb[:, 14:15], st_sb[:, 14:15], st_sb[:, 15:16], ALU.add), s_dve, 1)
            t_sq = b.op("vector", (lambda proj, sq_t: lambda e: e.scalar_tensor_tensor(
                sq_t, proj, 1.0, proj, ALU.mult, ALU.mult, accum_out=st_sb[:, 5:6]))(proj, sq_t), s_dve, 1)
            b.wait_own("vector", t_sq)
            t_c3 = b.op("vector", lambda e: e.tensor_tensor(st_sb[:, 4:5], st_sb[:, 12:13], st_sb[:, 14:15], ALU.add), s_dve, 1)
            t_v1 = b.op("vector", lambda e: e.tensor_scalar(st_sb[:, 5:6], st_sb[:, 5:6], 1.0 / D, EPS, ALU.mult, ALU.add), s_dve, 1)
            b.wait_own("vector", t_c3)
            t_m = b.op("vector", lambda e: e.tensor_scalar(st_sb[:, 4:5], st_sb[:, 4:5], 1.0 / D, 0.0, ALU.mult, ALU.add), s_dve, 1)
            b.wait_own("vector", t_m)
            t_m2 = b.op("vector", lambda e: e.tensor_tensor(st_sb[:, 6:7], st_sb[:, 4:5], st_sb[:, 4:5], ALU.mult), s_dve, 1)
            b.op("vector", (lambda proj: lambda e: e.tensor_scalar(proj, proj, st_sb[:, 4:5], 0.0, ALU.subtract, ALU.add))(proj))
            b.wait_own("vector", t_m2)
            t_var = b.op("vector", lambda e: e.tensor_tensor(st_sb[:, 5:6], st_sb[:, 5:6], st_sb[:, 6:7], ALU.subtract), s_dve, 1)
            b.wait("scalar", t_var)
            t_l1 = b.op("scalar", lambda e: e.activation(st_sb[:, 6:7], st_sb[:, 5:6], AF.Ln, bias=0.0, scale=np.float32(1.0)), s_act, 1)
            b.wait_own("scalar", t_l1)
            ta = b.op("scalar", lambda e: e.activation(st_sb[:, 7:8], st_sb[:, 6:7], AF.Exp, bias=0.0, scale=np.float32(-0.5)), s_act, 1)
            b.wait("vector", tmax([ta], [tlx], [yt_free[xb]]))
            ty = b.op("vector", (lambda xb, proj: lambda e: e.scalar_tensor_tensor(
                yt[xb], proj, st_sb[:, 7:8], xt[xb], ALU.mult, ALU.add,
                accum_out=st_sb[:, 8:9]))(xb, proj), s_dve, 1)
            xt_free3[xb] = ty
            b.wait("sync", ty)
            tsy = b.op("sync", (lambda tj, xb: lambda e: e.dma_start(
                out_h[tj * 128:(tj + 1) * 128, :], yt[xb]))(tj, xb), m_y[xb], 16)
            yt_free[xb] = tsy
            t_sq2 = b.op("vector", (lambda xb, sq_t: lambda e: e.scalar_tensor_tensor(
                sq_t, yt[xb], 1.0, yt[xb], ALU.mult, ALU.mult, accum_out=st_sb[:, 9:10]))(xb, sq_t), s_dve, 1)
            b.wait_own("vector", t_sq2)
            t_n1 = b.op("vector", lambda e: e.tensor_scalar(st_sb[:, 8:9], st_sb[:, 8:9], 1.0 / D, 0.0, ALU.mult, ALU.add), s_dve, 1)
            t_n2 = b.op("vector", lambda e: e.tensor_scalar(st_sb[:, 9:10], st_sb[:, 9:10], 1.0 / D, EPS, ALU.mult, ALU.add), s_dve, 1)
            b.wait_own("vector", t_n2)
            t_n3 = b.op("vector", lambda e: e.tensor_tensor(st_sb[:, 10:11], st_sb[:, 8:9], st_sb[:, 8:9], ALU.mult), s_dve, 1)
            b.wait_own("vector", t_n3)
            t_n4 = b.op("vector", lambda e: e.tensor_tensor(st_sb[:, 9:10], st_sb[:, 9:10], st_sb[:, 10:11], ALU.subtract), s_dve, 1)
            b.wait("scalar", t_n4)
            t_l2 = b.op("scalar", lambda e: e.activation(st_sb[:, 10:11], st_sb[:, 9:10], AF.Ln, bias=0.0, scale=np.float32(1.0)), s_act, 1)
            b.wait_own("scalar", t_l2)
            t_e2 = b.op("scalar", lambda e: e.activation(st_sb[:, 11:12], st_sb[:, 10:11], AF.Exp, bias=0.0, scale=np.float32(-0.5)), s_act, 1)
            b.wait("vector", t_e2)
            t_b = b.op("vector", lambda e: e.scalar_tensor_tensor(
                st_sb[:, 10:11], st_sb[:, 8:9], -1.0, st_sb[:, 11:12], ALU.mult, ALU.mult), s_dve, 1)
            b.wait_own("vector", t_b)
            tv2 = b.op("vector", (lambda xb, proj: lambda e: e.tensor_scalar(
                proj, yt[xb], st_sb[:, 11:12], st_sb[:, 10:11], ALU.mult, ALU.add))(xb, proj), s_dve, 1)
            if pend is not None:
                proj_free[pend[0] % 2] = emit_l2_transp(pend[0], p, pend[1])
            pend = (j, tv2)
        proj_free[pend[0] % 2] = emit_l2_transp(pend[0], p, pend[1])
        afm_free = mm_last
    l2st_free = l2_state["free"]

    tok_p3_ysync = tmax([b.final(s) for s in m_y])
    tok_p3_pe = (s_pe, b.counts["s_pe"])
    tok_p3_dve = (s_dve, b.counts["s_dve"])
    tok_p3_act = (s_act, b.counts["s_act"])

    if PHASES < 4:
        return _finish(nc, b)

    hdn = pbf[:, 0:45056]
    ln2fm = pbf[:, 45056:61440]
    gwp = [pbf[:, 61440:63488], pbf[:, 63488:65536]]
    uwp = [pbf[:, 65536:67584], pbf[:, 67584:69632]]
    dwp = [pbf[:, 69632 + i * 512: 69632 + (i + 1) * 512] for i in range(4)]
    silu_sb = [pf[:, i * 512:(i + 1) * 512] for i in range(4)]
    dsbc_sb = pf[:, 2048:4096]
    ffnst = [pf[:, 4096 + i * 512: 4096 + (i + 1) * 512] for i in range(4)]

    b.wait("sync", tmax([tok_p3_pe], [tok_p3_dve]))
    t_ds = b.op("sync", lambda e: e.dma_start(dsbc_sb, dsbc_h), m_const, 16)

    gwp_free = [tok_p3_pe, tok_p3_pe]
    uwp_free = [tok_p3_pe, tok_p3_pe]
    dwp_free = [None, None, None, None]
    ln2fm_free = tok_p3_pe
    ffnst_free = [tok_p3_pe] * 4
    hdn_free = None

    ft = [pf[:, 6144:8192], pf[:, 8192:10240]]
    ft_free = [tok_p3_dve, tok_p3_dve]
    ffn_done_tok = [None]

    def emit_epilogue_tile(pp, j):
        g = pp * 8 + j
        fb = j % 2
        b.wait("sync", tmax([ft_free[fb]], [b.final(s) for s in m_ffn] if j % 2 == 0 else None,
                            tok_p3_ysync if pp == 0 and j == 0 else None))
        tf = b.op("sync", (lambda g, fb: lambda e: e.dma_start(ft[fb], ffn_sc[g * 128:(g + 1) * 128, :]))(g, fb), m_ft[fb], 16)
        b.wait("vector", tf)
        ta5 = emit_ln(12, ft[fb], 10240)
        b.wait("vector", ta5)
        tapp = b.op("vector", (lambda fb: lambda e: e.tensor_scalar(ft[fb], ft[fb], st_sb[:, 15:16], 0.0, ALU.mult, ALU.add))(fb), s_dve, 1)
        b.wait("gpsimd", tmax([tapp], tok_p3_ysync if pp == 0 and j == 0 else None))
        tacc = b.op("gpsimd", (lambda g, fb: lambda e: e.dma_start(
            out_h[g * 128:(g + 1) * 128, :], ft[fb], accum_op=ALU.add))(g, fb), m_out[fb], 16)
        ft_free[fb] = tacc

    for p in range(4):
        b.wait("sync", tmax([ln2fm_free], [l2st_free]))
        tok_l2 = None
        for hc in range(NDC):
            tok_l2 = b.op("sync", (lambda hc, p: lambda e: e.dma_start(
                ln2fm[:, hc * 1024:(hc + 1) * 1024], l2_sc[hc][:, p * 1024:(p + 1) * 1024]))(hc, p), m_l2f, 16)
        hdn_last = None
        gu_last = None
        for ip in range(NIC):
            wb = ip % 2
            b.wait("gpsimd", gwp_free[wb])
            tg = b.op("gpsimd", (lambda ip, wb: lambda e: e.dma_start(
                gwp[wb].rearrange("p (hc i) -> p hc i", i=128), gw_h[ip]))(ip, wb), m_gw[wb], 16)
            b.wait("gpsimd", uwp_free[wb])
            tu = b.op("gpsimd", (lambda ip, wb: lambda e: e.dma_start(
                uwp[wb].rearrange("p (hc i) -> p hc i", i=128), uw_h[ip]))(ip, wb), m_uw[wb], 16)
            bbase = (ip % 2) * 4
            b.wait("tensor", tmax([tg], [tu], [tok_l2], [hdn_free] if ip == 0 else None,
                                  [bankfree[bbase]], [bankfree[bbase + 1]],
                                  [bankfree[bbase + 2]], [bankfree[bbase + 3]]))
            tk = None
            for hc in range(NDC):
                stt_ = hc == 0
                sp = hc == NDC - 1
                for k in range(2):
                    b.op("tensor", (lambda wb, hc, k, bbase, stt_, sp: lambda e: e.matmul(
                        ps[bbase + k], gwp[wb][:, hc * 128:(hc + 1) * 128],
                        ln2fm[:, hc * 1024 + k * 512: hc * 1024 + (k + 1) * 512],
                        start=stt_, stop=sp))(wb, hc, k, bbase, stt_, sp))
                for k in range(2):
                    last = sp and k == 1
                    tk = b.op("tensor", (lambda wb, hc, k, bbase, stt_, sp: lambda e: e.matmul(
                        ps[bbase + 2 + k], uwp[wb][:, hc * 128:(hc + 1) * 128],
                        ln2fm[:, hc * 1024 + k * 512: hc * 1024 + (k + 1) * 512],
                        start=stt_, stop=sp))(wb, hc, k, bbase, stt_, sp),
                        s_pe if last else None, 1 if last else 0)
            gu_last = tk
            gwp_free[wb] = tk
            uwp_free[wb] = tk
            for k in range(2):
                sb4 = (ip % 2) * 2 + k
                b.wait("scalar", tk)
                tsl = b.op("scalar", (lambda sb4, bbase, k, ip: lambda e: e.activation(
                    silu_sb[sb4], ps[bbase + k], AF.Silu, scale=gs_sb[:, ip:ip + 1]))(sb4, bbase, k, ip), s_act, 1)
                bankfree[bbase + k] = tsl
                b.wait("vector", tsl)
                th = b.op("vector", (lambda sb4, bbase, k, ip: lambda e: e.scalar_tensor_tensor(
                    hdn[:, ip * 1024 + k * 512: ip * 1024 + (k + 1) * 512],
                    ps[bbase + 2 + k], us_sb[:, ip:ip + 1], silu_sb[sb4], ALU.mult, ALU.mult))(sb4, bbase, k, ip), s_dve, 1)
                bankfree[bbase + 2 + k] = th
                hdn_last = th
        tok_hdn = hdn_last
        ln2fm_free = gu_last
        if DBG and p == 0:
            b.wait("sync", tok_hdn)
            b.op("sync", lambda e: e.dma_start(hdn_dbg, hdn), m_dbg, 16)
        down_last = None
        for o in range(4):
            for ip in range(NIC):
                wb = ip % 4
                b.wait("sync", tmax([dwp_free[wb]], tok_dwprep if p == 0 and o == 0 and ip < 4 else None))
                td = b.op("sync", (lambda ip, o, wb: lambda e: e.dma_start(
                    dwp[wb], dwbf_sc[ip][:, o * 512:(o + 1) * 512]))(ip, o, wb), m_dwp[wb], 16)
                if ip == 0:
                    b.wait("tensor", tmax([tok_hdn], *[[bankfree[i]] for i in range(8)]))
                b.wait("tensor", td)
                for j in range(8):
                    last = (j == 7)
                    tk = b.op("tensor", (lambda wb, ip, j: lambda e: e.matmul(
                        ps[j], hdn[:, ip * 1024 + j * 128: ip * 1024 + (j + 1) * 128],
                        dwp[wb], start=(ip == 0), stop=(ip == NIC - 1)))(wb, ip, j),
                        s_pe if last else None, 1 if last else 0)
                dwp_free[wb] = tk
            down_last = tk
            for j in range(8):
                fb = j % 4
                b.wait("vector", tmax([down_last], [t_ds], [ffnst_free[fb]]))
                tvd = b.op("vector", (lambda j, o, fb: lambda e: e.tensor_tensor(
                    ffnst[fb], ps[j], dsbc_sb[:, o * 512:(o + 1) * 512], ALU.mult))(j, o, fb), s_dve, 1)
                bankfree[j] = tvd
                b.wait("sync", tvd)
                tj = p * 8 + j
                tsf = b.op("sync", (lambda tj, o, fb: lambda e: e.dma_start(
                    ffn_sc[tj * 128:(tj + 1) * 128, o * 512:(o + 1) * 512], ffnst[fb]))(tj, o, fb), m_ffn[fb], 16)
                ffnst_free[fb] = tsf
            if p > 0:
                emit_epilogue_tile(p - 1, 2 * o)
                emit_epilogue_tile(p - 1, 2 * o + 1)
        hdn_free = down_last
        ffn_done_tok[0] = tmax([b.final(s) for s in m_ffn])

    for j in range(8):
        emit_epilogue_tile(3, j)
    tok_p4_sync = tmax([b.final(s) for s in m_ffn])

    return _finish(nc, b)

def _finish(nc, b):
    for name in b.semnames:
        if b.own[name] is None and b.counts[name] > 0:
            b.q["sync"].append(("w", name, b.counts[name]))
    stack = contextlib.ExitStack()
    b.emit(nc, stack)
    stack.close()
    return nc


def stage_inputs(L0_attn_Wqkv, L0_attn_bqkv, L0_attn_Wo, L0_attn_bo,
                 L0_ffn_gate_w, L0_ffn_up_w, L0_ffn_down_w,
                 L0_ffn_gate_s, L0_ffn_up_s, L0_ffn_down_s):
    f32 = lambda a: np.ascontiguousarray(np.asarray(a, dtype=np.float32))
    Wqkv = f32(L0_attn_Wqkv)
    sh = {}
    sh["wqkv_sw"] = np.ascontiguousarray(Wqkv.reshape(NDC, 128, 12, 512).transpose(2, 1, 0, 3))
    bq = f32(L0_attn_bqkv)
    sh["bqkv_t"] = np.ascontiguousarray(bq.reshape(48, 128).T)
    sh["bv_bc"] = np.ascontiguousarray(np.broadcast_to(bq[4096:6144], (128, D)))
    sh["wo_n"] = f32(L0_attn_Wo)
    sh["bo_bc"] = np.ascontiguousarray(np.broadcast_to(f32(L0_attn_bo), (128, D)))
    sh["gw_sw"] = np.ascontiguousarray(f32(L0_ffn_gate_w).reshape(NIC, 128, NDC, 128).transpose(0, 3, 2, 1))
    sh["uw_sw"] = np.ascontiguousarray(f32(L0_ffn_up_w).reshape(NIC, 128, NDC, 128).transpose(0, 3, 2, 1))
    sh["dw_t"] = np.ascontiguousarray(f32(L0_ffn_down_w).T.reshape(NIC, 128, D))
    sh["gs_t"] = np.ascontiguousarray(f32(L0_ffn_gate_s).reshape(NIC, 128).T)
    sh["us_t"] = np.ascontiguousarray(f32(L0_ffn_up_s).reshape(NIC, 128).T)
    sh["ds_bc"] = np.ascontiguousarray(np.broadcast_to(f32(L0_ffn_down_s), (128, D)))
    sl, tl = np.arange(128)[:, None], np.arange(512)[None, :]
    masks = np.stack([(tl >= sl + 128 * k) for k in range(4)], 0).astype(ml_dtypes.bfloat16)
    sh["masks"] = np.ascontiguousarray(masks.transpose(1, 0, 2).reshape(128, 2048))
    sh["ident_f32"] = np.eye(128, dtype=np.float32)
    sh["ones_f32"] = np.ones((1, 128), dtype=np.float32)
    return sh


def kernel(L0_attn_x, L0_attn_Wqkv, L0_attn_bqkv, L0_attn_Wo, L0_attn_bo,
           L0_ffn_gate_w, L0_ffn_up_w, L0_ffn_down_w,
           L0_ffn_gate_s, L0_ffn_up_s, L0_ffn_down_s):
    x = np.ascontiguousarray(np.asarray(L0_attn_x, dtype=np.float32))
    sh = stage_inputs(L0_attn_Wqkv, L0_attn_bqkv, L0_attn_Wo, L0_attn_bo,
                      L0_ffn_gate_w, L0_ffn_up_w, L0_ffn_down_w,
                      L0_ffn_gate_s, L0_ffn_up_s, L0_ffn_down_s)
    nc = build_nc()
    in_maps = [dict(sh, x=np.ascontiguousarray(x[c])) for c in range(8)]
    res = run_bass_kernel_spmd(nc, in_maps, core_ids=list(range(8)))
    return np.stack([r["out"] for r in res.results], axis=0)


# version 17
# speedup vs baseline: 1.0246x; 1.0041x over previous
import contextlib
import numpy as np
import ml_dtypes
import concourse.bass as bass
import concourse.mybir as mybir
from concourse.bass_utils import run_bass_kernel_spmd

F32 = mybir.dt.float32
BF16 = mybir.dt.bfloat16
AF = mybir.ActivationFunctionType
AX = mybir.AxisListType
ALU = mybir.AluOpType

S = 4096
D = 2048
NDC = 16
NIC = 44
NH = 16
SCALE = np.float32(1.0 / np.sqrt(128.0))
ESHIFT = np.float32(-8.0)
EPS = 1e-5

ENGS = ("sync", "gpsimd", "tensor", "scalar", "vector")
PHASES = 5
DBG = False


def tmax(*toks):
    out = {}

    def add(t):
        if t is None:
            return
        if isinstance(t, list):
            for u in t:
                add(u)
        else:
            s, v = t
            if v > out.get(s, -1):
                out[s] = v
    for t in toks:
        add(t)
    return [(s, v) for s, v in out.items()] or None


class Builder:
    def __init__(self):
        self.q = {e: [] for e in ENGS}
        self.semnames = []
        self.counts = {}
        self.waited = {e: {} for e in ENGS}
        self.own = {}

    def sem(self, name, owner):
        self.semnames.append(name)
        self.counts[name] = 0
        self.own[name] = owner
        return name

    def wait(self, eng, tok):
        if tok is None:
            return
        if isinstance(tok, list):
            for t in tmax(tok):
                self._wait1(eng, t)
            return
        self._wait1(eng, tok)

    def _wait1(self, eng, tok):
        s, v = tok
        if v <= 0:
            return
        if self.own[s] == eng:
            return
        if self.waited[eng].get(s, -1) >= v:
            return
        self.waited[eng][s] = v
        self.q[eng].append(("w", s, v))

    def wait_own(self, eng, tok):
        s, v = tok
        if v <= 0 or self.waited[eng].get(s, -1) >= v:
            return
        self.waited[eng][s] = v
        self.q[eng].append(("w", s, v))

    def op(self, eng, fn, sem=None, inc=0):
        self.q[eng].append(("o", fn, sem, inc))
        if sem is not None:
            self.counts[sem] += inc
            return (sem, self.counts[sem])
        return None

    def final(self, sem):
        return (sem, self.counts[sem])

    def emit(self, nc, stack):
        hs = {n: stack.enter_context(nc.semaphore(n)) for n in self.semnames}
        with nc.Block() as block:
            for eng in ENGS:
                items = self.q[eng]

                def make(items):
                    def body(e):
                        for it in items:
                            if it[0] == "w":
                                e.wait_ge(hs[it[1]], it[2])
                            else:
                                ins = it[1](e)
                                if it[2] is not None:
                                    ins.then_inc(hs[it[2]], it[3])
                    return body

                getattr(block, eng)(make(items))


def build_nc():
    nc = bass.Bass("TRN2", target_bir_lowering=False)
    b = Builder()
    s_pe = b.sem("s_pe", "tensor")
    s_act = b.sem("s_act", "scalar")
    s_dve = b.sem("s_dve", "vector")
    s_gp = b.sem("s_gp", "gpsimd")
    ds = lambda name, n: [b.sem("%s%d" % (name, i), None) for i in range(n)]
    m_const = b.sem("m_const", None)
    m_x = ds("m_x", 2)
    m_wq = ds("m_wq", 2)
    m_qkst = ds("m_qkst", 2)
    m_vst = ds("m_vst", 2)
    m_head = ds("m_head", 2)
    m_af = ds("m_af", 2)
    m_dwl = ds("m_dwl", 2)
    m_dws = ds("m_dws", 2)
    m_afm = b.sem("m_afm", None)
    m_wo = b.sem("m_wo", None)
    m_y = ds("m_y", 2)
    m_l2st = b.sem("m_l2st", None)
    m_l2f = b.sem("m_l2f", None)
    m_gw = ds("m_gw", 2)
    m_uw = ds("m_uw", 2)
    m_dwp = ds("m_dwp", 4)
    m_ffn = ds("m_ffn", 4)
    m_ft = ds("m_ft", 2)
    m_y5 = ds("m_y5", 2)
    m_out = ds("m_out", 2)
    m_t1 = b.sem("m_t1", None)
    m_t2 = b.sem("m_t2", None)
    m_t1 = [ds("m_t1a", 2), ds("m_t1b", 2)]
    m_dbg = b.sem("m_dbg", None)

    x_h = nc.dram_tensor("x", [S, D], F32, kind="ExternalInput").ap()
    wqkv_h = nc.dram_tensor("wqkv_sw", [12, 128, NDC, 512], F32, kind="ExternalInput").ap()
    bqkv_h = nc.dram_tensor("bqkv_t", [128, 48], F32, kind="ExternalInput").ap()
    bvbc_h = nc.dram_tensor("bv_bc", [128, D], F32, kind="ExternalInput").ap()
    wo_h = nc.dram_tensor("wo_n", [D, D], F32, kind="ExternalInput").ap()
    bobc_h = nc.dram_tensor("bo_bc", [128, D], F32, kind="ExternalInput").ap()
    gw_h = nc.dram_tensor("gw_sw", [NIC, 128, NDC, 128], F32, kind="ExternalInput").ap()
    uw_h = nc.dram_tensor("uw_sw", [NIC, 128, NDC, 128], F32, kind="ExternalInput").ap()
    dw_h = nc.dram_tensor("dw_t", [NIC, 128, D], F32, kind="ExternalInput").ap()
    gs_h = nc.dram_tensor("gs_t", [128, NIC], F32, kind="ExternalInput").ap()
    us_h = nc.dram_tensor("us_t", [128, NIC], F32, kind="ExternalInput").ap()
    dsbc_h = nc.dram_tensor("ds_bc", [128, D], F32, kind="ExternalInput").ap()
    mask_h = nc.dram_tensor("masks", [128, 2048], BF16, kind="ExternalInput").ap()
    idf_h = nc.dram_tensor("ident_f32", [128, 128], F32, kind="ExternalInput").ap()
    onef_h = nc.dram_tensor("ones_f32", [1, 128], F32, kind="ExternalInput").ap()
    out_h = nc.dram_tensor("out", [S, D], F32, kind="ExternalOutput").ap()

    KI = "ExternalOutput" if DBG else "Internal"
    q_sc = nc.dram_tensor("q_sc", [NH, 128, S], BF16, kind=KI).ap()
    k_sc = nc.dram_tensor("k_sc", [NH, 128, S], BF16, kind=KI).ap()
    v_sc = nc.dram_tensor("v_sc", [NH, 32, 128, 128], BF16, kind=KI).ap()
    af_sc = nc.dram_tensor("af_sc", [NDC, 128, S], BF16, kind=KI).ap()
    l2_sc = nc.dram_tensor("l2_sc", [NDC, 128, S], BF16, kind=KI).ap()
    dwbf_sc = nc.dram_tensor("dwbf_sc", [NIC, 128, D], BF16, kind=KI).ap()
    ffn_sc = nc.dram_tensor("ffn_sc", [S, D], F32, kind=KI).ap()
    hdn_dbg = nc.dram_tensor("hdn_dbg", [128, 45056], BF16, kind=KI).ap() if DBG else None

    pbf = nc.alloc_sbuf_tensor("pool_bf", [128, 71680], BF16).ap()
    pf = nc.alloc_sbuf_tensor("pool_f32", [128, 14336], F32).ap()
    mask_sb = nc.alloc_sbuf_tensor("mask_sb", [128, 2048], BF16).ap()
    idf_sb = nc.alloc_sbuf_tensor("idf_sb", [128, 128], F32).ap()
    onef_sb = nc.alloc_sbuf_tensor("onef_sb", [1, 128], F32).ap()
    bqkv_sb = nc.alloc_sbuf_tensor("bqkv_sb", [128, 48], F32).ap()
    gs_sb = nc.alloc_sbuf_tensor("gs_sb", [128, NIC], F32).ap()
    us_sb = nc.alloc_sbuf_tensor("us_sb", [128, NIC], F32).ap()
    st_sb = nc.alloc_sbuf_tensor("st_sb", [128, 16], F32).ap()
    iz_sb = nc.alloc_sbuf_tensor("iz_sb", [1, 1024], F32).ap()
    zbc_sb = nc.alloc_sbuf_tensor("zbc_sb", [128, 512], F32).ap()

    ps = [nc.alloc_psum_tensor("ps%d" % i, [128, 512], F32).ap() for i in range(8)]
    bankfree = [None] * 8

    b.op("sync", lambda e: e.dma_start(mask_sb[:], mask_h), m_const, 16)
    b.op("sync", lambda e: e.dma_start(idf_sb[:], idf_h), m_const, 16)
    b.op("sync", lambda e: e.dma_start(onef_sb[:], onef_h), m_const, 16)
    b.op("sync", lambda e: e.dma_start(bqkv_sb[:], bqkv_h), m_const, 16)
    b.op("sync", lambda e: e.dma_start(gs_sb[:], gs_h), m_const, 16)
    tok_const = b.op("sync", lambda e: e.dma_start(us_sb[:], us_h), m_const, 16)

    def emit_ln(c, tile, sq_off):
        b.op("vector", (lambda c, tile: lambda e: e.reduce_sum(st_sb[:, c:c + 1], tile, axis=AX.X))(c, tile), s_dve, 1)
        t2 = b.op("vector", (lambda c, tile, sq_off: lambda e: e.scalar_tensor_tensor(
            pf[:, sq_off:sq_off + D], tile, 1.0, tile, ALU.mult, ALU.mult,
            accum_out=st_sb[:, c + 1:c + 2]))(c, tile, sq_off), s_dve, 1)
        b.wait_own("vector", t2)
        b.op("vector", (lambda c: lambda e: e.tensor_scalar(st_sb[:, c:c + 1], st_sb[:, c:c + 1], 1.0 / D, 0.0, ALU.mult, ALU.add))(c), s_dve, 1)
        t4 = b.op("vector", (lambda c: lambda e: e.tensor_scalar(st_sb[:, c + 1:c + 2], st_sb[:, c + 1:c + 2], 1.0 / D, EPS, ALU.mult, ALU.add))(c), s_dve, 1)
        b.wait_own("vector", t4)
        t5 = b.op("vector", (lambda c: lambda e: e.tensor_tensor(st_sb[:, c + 2:c + 3], st_sb[:, c:c + 1], st_sb[:, c:c + 1], ALU.mult))(c), s_dve, 1)
        b.op("vector", (lambda c, tile: lambda e: e.tensor_scalar(tile, tile, st_sb[:, c:c + 1], 0.0, ALU.subtract, ALU.add))(c, tile))
        b.wait_own("vector", t5)
        t7 = b.op("vector", (lambda c: lambda e: e.tensor_tensor(st_sb[:, c + 1:c + 2], st_sb[:, c + 1:c + 2], st_sb[:, c + 2:c + 3], ALU.subtract))(c), s_dve, 1)
        b.wait("scalar", t7)
        t8 = b.op("scalar", (lambda c: lambda e: e.activation(st_sb[:, c + 2:c + 3], st_sb[:, c + 1:c + 2], AF.Ln, bias=0.0, scale=np.float32(1.0)))(c), s_act, 1)
        b.wait_own("scalar", t8)
        ta = b.op("scalar", (lambda c: lambda e: e.activation(st_sb[:, c + 3:c + 4], st_sb[:, c + 2:c + 3], AF.Exp, bias=0.0, scale=np.float32(-0.5)))(c), s_act, 1)
        return ta

    ln1fm = [pbf[:, 0:16384], pbf[:, 16384:32768]]
    wqp = [pbf[:, 32768:40960], pbf[:, 40960:49152]]
    qkst = [pbf[:, 49152:51200], pbf[:, 51200:53248]]
    vst = [pbf[:, 53248:53760], pbf[:, 53760:54272]]
    lnb = [pbf[:, 57344:59392], pbf[:, 59392:61440]]
    xt = [pf[:, 0:2048], pf[:, 2048:4096]]
    bvbc_sb = pf[:, 8192:10240]

    t_bv = b.op("sync", lambda e: e.dma_start(bvbc_sb, bvbc_h), m_const, 16)
    b.wait("tensor", t_bv)
    b.wait("vector", t_bv)
    b.wait("scalar", t_bv)

    xt_free = [None, None]
    wqp_free = [None, None]
    qkst_free = [None, None]
    vst_free = [None, None]
    ln1fm_ready = [[None, None], [None, None]]
    ln1fm_free = [None, None]
    lnb_free1 = [None, None]

    for p in range(4):
        par = p % 2
        last_evac = None
        for j in range(8):
            tj = p * 8 + j
            xb = j % 2
            b.wait("scalar", xt_free[xb])
            tl = b.op("scalar", (lambda tj, xb: lambda e: e.dma_start(xt[xb], x_h[tj * 128:(tj + 1) * 128, :]))(tj, xb), m_x[xb], 16)
            b.wait("vector", tmax([tl]))
            ta = emit_ln(0, xt[xb], 6144)
            b.wait("vector", tmax([ta], [lnb_free1[xb]]))
            tv = b.op("vector", (lambda xb: lambda e: e.tensor_scalar(lnb[xb], xt[xb], st_sb[:, 3:4], 0.0, ALU.mult, ALU.add))(xb), s_dve, 1)
            xt_free[xb] = tv
            b.wait("scalar", tmax([tv], [ln1fm_free[par]] if j == 0 else None))
            for dc in range(NDC):
                last_evac = b.op("scalar", (lambda par, dc, j, xb: lambda e: e.dma_start(
                    ln1fm[par][:, dc * 1024 + j * 128: dc * 1024 + (j + 1) * 128],
                    lnb[xb][:, dc * 128:(dc + 1) * 128], transpose=True))(par, dc, j, xb),
                    m_t1[par][j // 4], 16)
            lnb_free1[xb] = last_evac
            if j == 3:
                ln1fm_ready[par][0] = last_evac
        ln1fm_ready[par][1] = last_evac

        mmtok = None
        for pn in range(12):
            wb = pn % 2
            b.wait("gpsimd", tmax([wqp_free[wb]], [tok_const] if p == 0 and pn < 2 else None))
            tw = b.op("gpsimd", (lambda pn, wb: lambda e: e.dma_start(
                wqp[wb].rearrange("p (dc o) -> p dc o", o=512), wqkv_h[pn]))(pn, wb), m_wq[wb], 16)
            if pn < 8:
                for k in range(2):
                    sb = (pn * 2 + k) % 2
                    evtoks = []
                    for oc in range(4):
                        bk = (pn * 8 + k * 4 + oc) % 4
                        b.wait("tensor", tmax([tw], [ln1fm_ready[par][k]], [bankfree[bk]]))
                        for dc in range(NDC):
                            tk = b.op("tensor", (lambda wb, dc, oc, bk, par, k: lambda e: e.matmul(
                                ps[bk], wqp[wb][:, dc * 512 + oc * 128: dc * 512 + (oc + 1) * 128],
                                ln1fm[par][:, dc * 1024 + k * 512: dc * 1024 + (k + 1) * 512],
                                start=(dc == 0), stop=(dc == NDC - 1)))(wb, dc, oc, bk, par, k),
                                s_pe if dc == NDC - 1 else None, 1 if dc == NDC - 1 else 0)
                        mmtok = tk
                        cglob = pn * 4 + oc
                        b.wait("scalar", tmax([tk], [qkst_free[sb]] if oc == 0 else None))
                        tac = b.op("scalar", (lambda sb, oc, bk, cglob: lambda e: e.activation(
                            qkst[sb][:, oc * 512:(oc + 1) * 512], ps[bk], AF.Identity,
                            bias=bqkv_sb[:, cglob:cglob + 1]))(sb, oc, bk, cglob), s_act, 1)
                        bankfree[bk] = tac
                        evtoks.append(tac)
                    b.wait("sync", evtoks)
                    dest = q_sc if pn < 4 else k_sc
                    h0 = (pn * 4) % 16
                    t0 = p * 1024 + k * 512
                    tst = b.op("sync", (lambda dest, h0, t0, sb: lambda e: e.dma_start(
                        dest[h0:h0 + 4, :, t0:t0 + 512].rearrange("h p t -> p h t"),
                        qkst[sb].rearrange("p (h t) -> p h t", h=4)))(dest, h0, t0, sb), m_qkst[sb], 16)
                    qkst_free[sb] = tst
            else:
                for j in range(8):
                    tj = p * 8 + j
                    sb = (pn * 8 + j) % 2
                    bk = (pn * 8 + j) % 4
                    b.wait("tensor", tmax([tw], [ln1fm_ready[par][j // 4]], [bankfree[bk]]))
                    for dc in range(NDC):
                        tk = b.op("tensor", (lambda wb, dc, j, bk, par: lambda e: e.matmul(
                            ps[bk], ln1fm[par][:, dc * 1024 + j * 128: dc * 1024 + (j + 1) * 128],
                            wqp[wb][:, dc * 512:(dc + 1) * 512],
                            start=(dc == 0), stop=(dc == NDC - 1)))(wb, dc, j, bk, par),
                            s_pe if dc == NDC - 1 else None, 1 if dc == NDC - 1 else 0)
                    mmtok = tk
                    b.wait("vector", tmax([tk], [vst_free[sb]], [t_bv]))
                    tvv = b.op("vector", (lambda sb, bk, pn: lambda e: e.scalar_tensor_tensor(
                        vst[sb], ps[bk], 1.0, bvbc_sb[:, (pn - 8) * 512:(pn - 7) * 512],
                        ALU.mult, ALU.add))(sb, bk, pn), s_dve, 1)
                    bankfree[bk] = tvv
                    b.wait("sync", tvv)
                    h0 = (pn - 8) * 4
                    tsv = b.op("sync", (lambda sb, h0, tj: lambda e: e.dma_start(
                        v_sc[h0:h0 + 4, tj, :, :].rearrange("h p w -> p h w"),
                        vst[sb].rearrange("p (h w) -> p h w", w=128)))(sb, h0, tj), m_vst[sb], 16)
                    vst_free[sb] = tsv
            wqp_free[wb] = mmtok
        ln1fm_free[par] = mmtok

    tok_p1_sync = tmax([b.final(s) for s in m_qkst + m_vst])
    tok_p1_pe = (s_pe, b.counts["s_pe"])

    if PHASES < 2:
        return _finish(nc, b)

    qb = [pbf[:, 0:4096], pbf[:, 4096:8192]]
    kb = [pbf[:, 8192:12288], pbf[:, 12288:16384]]
    pbuf = [pbf[:, 16384:32768], pbf[:, 46336:62720]]
    va_view = [pbf[:, 32768 + k * 4224: 32768 + (k + 1) * 4224].rearrange("p (c w) -> p c w", w=132) for k in range(2)]
    afst = [pbf[:, 41216:41728], pbf[:, 41728:42240]]
    dwc = [pbf[:, 42240:44288], pbf[:, 44288:46336]]

    b.wait("vector", tok_p1_pe)
    tok_ones = []
    for k2 in range(2):
        tok_ones.append(b.op("vector", (lambda k2: lambda e: e.memset(va_view[k2][:, :, 128:129], 1.0))(k2), s_dve, 1))

    pe_head_last = [tok_p1_pe, tok_p1_pe]
    afst_free = [tok_p1_pe, tok_p1_pe]
    pbuf_free = [None, None]
    iz_free = [None, None]
    dwc_store = [None, None]
    dw_ip = [0]
    head_loads = [None, None]

    def emit_head_loads(h):
        hb = h % 2
        b.wait("sync", tmax(tok_p1_sync, [pe_head_last[hb]], tok_ones if h < 2 else None))
        b.op("sync", (lambda h, hb: lambda e: e.dma_start(qb[hb], q_sc[h]))(h, hb), m_head[hb], 16)
        b.op("sync", (lambda h, hb: lambda e: e.dma_start(kb[hb], k_sc[h]))(h, hb), m_head[hb], 16)
        ld = None
        for vq in range(4):
            ld = b.op("sync", (lambda h, hb, vq: lambda e: e.dma_start(
                va_view[hb][:, vq * 8:(vq + 1) * 8, 0:128],
                v_sc[h][vq * 8:(vq + 1) * 8].rearrange("c p w -> p c w")))(h, hb, vq), m_head[hb], 16)
        head_loads[hb] = ld
        for _ in range(3):
            ip = dw_ip[0]
            if ip < NIC:
                cb = ip % 2
                b.wait("gpsimd", tmax([tok_p1_pe] if ip < 2 else None, [dwc_store[cb]]))
                tld = b.op("gpsimd", (lambda ip, cb: lambda e: e.dma_start(dwc[cb], dw_h[ip]))(ip, cb), m_dwl[cb], 16)
                b.wait("sync", tld)
                dwc_store[cb] = b.op("sync", (lambda ip, cb: lambda e: e.dma_start(dwbf_sc[ip], dwc[cb]))(ip, cb), m_dws[cb], 16)
                dw_ip[0] += 1

    def emit_pv_chunk(u, c):
        hb, par, nk = u["hb"], u["par"], u["nk"]
        bo_, bz = 2 + 3 * par, 3 + 3 * par
        b.wait("tensor", tmax([u["exp"][c]],
                              [bankfree[bo_], bankfree[bz]] if c == 0 else None))
        pb = pbuf[par]
        b.op("tensor", (lambda hb, c, st, sp, bo_, pb: lambda e: e.matmul(
            ps[bo_], va_view[hb][:, c, 0:128], pb[:, c * 512:(c + 1) * 512],
            start=st, stop=sp))(hb, c, c == 0, c == nk - 1, bo_, pb))
        tz = b.op("tensor", (lambda hb, c, st, sp, bz, pb: lambda e: e.matmul(
            ps[bz][0:1, :], va_view[hb][:, c, 128:129], pb[:, c * 512:(c + 1) * 512],
            start=st, stop=sp))(hb, c, c == 0, c == nk - 1, bz, pb), s_pe, 1)
        return tz

    def emit_recip(u, pv_last):
        par = u["par"]
        bz = 3 + 3 * par
        b.wait("vector", tmax([pv_last], [iz_free[par]]))
        trc = b.op("vector", (lambda bz, par: lambda e: e.reciprocal(
            iz_sb[0:1, par * 512:(par + 1) * 512], ps[bz][0:1, :]))(bz, par), s_dve, 1)
        bankfree[bz] = trc
        u["trc"] = trc

    def emit_evacs(u):
        par = u["par"]
        bo_, bb_ = 2 + 3 * par, 4 + 3 * par
        b.wait("tensor", tmax([u["trc"]], [bankfree[bb_]]))
        tbc = b.op("tensor", (lambda bb_, par: lambda e: e.matmul(
            ps[bb_], onef_sb, iz_sb[0:1, par * 512:(par + 1) * 512], start=True, stop=True))(bb_, par), s_pe, 1)
        iz_free[par] = tbc
        b.wait("vector", tbc)
        tcp = b.op("vector", (lambda bb_: lambda e: e.tensor_copy(zbc_sb, ps[bb_]))(bb_), s_dve, 1)
        bankfree[bb_] = tcp
        ab = u["ab"]
        b.wait("vector", afst_free[ab])
        tstt = b.op("vector", (lambda bo_, ab: lambda e: e.scalar_tensor_tensor(
            afst[ab], ps[bo_], 1.0, zbc_sb, ALU.mult, ALU.mult))(bo_, ab), s_dve, 1)
        bankfree[bo_] = tstt
        b.wait("sync", tstt)
        tsa = b.op("sync", (lambda u, ab: lambda e: e.dma_start(
            af_sc[u["h"], :, u["tq"] * 512:(u["tq"] + 1) * 512], afst[ab]))(u, ab), m_af[ab], 16)
        afst_free[ab] = tsa

    prev = None
    pend_evac = []
    for i in range(NH * 8):
        h, tq = i // 8, i % 8
        if i == 0:
            emit_head_loads(0)
        if tq == 1 and h + 1 < NH:
            emit_head_loads(h + 1)
        hb = h % 2
        par = i % 2
        nk = 4 * (tq + 1)
        u = {"h": h, "tq": tq, "hb": hb, "par": par, "nk": nk, "exp": {}, "ab": i % 2}
        if len(pend_evac) == 1:
            emit_evacs(pend_evac.pop(0))
        prev_nk = prev["nk"] if prev else 0
        pv_last = None
        for c in range(max(nk, prev_nk)):
            if c < nk:
                bk = c % 2
                b.wait("tensor", tmax([head_loads[hb]], [bankfree[bk]]))
                tsc = b.op("tensor", (lambda hb, c, tq, bk: lambda e: e.matmul(
                    ps[bk], kb[hb][:, c * 128:(c + 1) * 128], qb[hb][:, tq * 512:(tq + 1) * 512],
                    start=True, stop=True))(hb, c, tq, bk), s_pe, 1)
                b.wait("scalar", tmax([tsc], [pbuf_free[par]] if c == 0 else None))
                te = b.op("scalar", (lambda c, bk, par: lambda e: e.activation(
                    pbuf[par][:, c * 512:(c + 1) * 512], ps[bk], AF.Exp, bias=ESHIFT, scale=SCALE))(c, bk, par), s_act, 1)
                bankfree[bk] = te
                if c >= 4 * tq:
                    kk = c - 4 * tq
                    b.wait("gpsimd", te)
                    te = b.op("gpsimd", (lambda c, kk, par: lambda e: e.tensor_tensor(
                        pbuf[par][:, c * 512:(c + 1) * 512], pbuf[par][:, c * 512:(c + 1) * 512],
                        mask_sb[:, kk * 512:(kk + 1) * 512], ALU.mult))(c, kk, par), s_gp, 1)
                u["exp"][c] = te
            if prev is not None and c < prev_nk:
                pv_last = emit_pv_chunk(prev, c)
        if prev is not None:
            pbuf_free[prev["par"]] = pv_last
            if prev["tq"] == 7:
                pe_head_last[prev["hb"]] = pv_last
            emit_recip(prev, pv_last)
            pend_evac.append(prev)
        prev = u
    pv_last = None
    for c in range(prev["nk"]):
        pv_last = emit_pv_chunk(prev, c)
    pbuf_free[prev["par"]] = pv_last
    pe_head_last[prev["hb"]] = pv_last
    emit_recip(prev, pv_last)
    pend_evac.append(prev)
    for uu in pend_evac:
        emit_evacs(uu)

    tok_p2_sync = tmax([b.final(s) for s in m_af])
    tok_p2_pe = (s_pe, b.counts["s_pe"])
    tok_p2_act = (s_act, b.counts["s_act"])
    tok_p2_dve = (s_dve, b.counts["s_dve"])
    tok_dwprep = tmax(dwc_store)

    if PHASES < 3:
        return _finish(nc, b)

    afm = pbf[:, 0:16384]
    wo_sb = pbf[:, 16384:49152]
    l2st = pbf[:, 49152:57344]
    projs = [pf[:, 4096:6144], pf[:, 10240:12288]]
    yt = [pf[:, 6144:8192], pf[:, 8192:10240]]
    sqo = [6144, 8192]
    bobc_sb = pf[:, 12288:14336]
    yt_free = [tok_p2_dve, tok_p2_dve]

    twl = None
    for dc in range(NDC):
        b.wait("gpsimd", tmax([tok_p2_pe], [tok_p2_act], [tok_p2_dve], tok_dwprep))
        twl = b.op("gpsimd", (lambda dc: lambda e: e.dma_start(
            wo_sb[:, dc * 2048:(dc + 1) * 2048], wo_h[dc * 128:(dc + 1) * 128, :]))(dc), m_wo, 16)
    tok_wo = twl
    b.wait("sync", tmax([tok_p2_pe], [tok_p2_dve]))
    t_bo = b.op("sync", lambda e: e.dma_start(bobc_sb, bobc_h), m_const, 16)

    afm_free = tok_p2_pe
    l2st_free = None
    xt_free3 = [xt_free[0], xt_free[1]]
    proj_free = [None, None]
    l2_state = {"free": None}

    def emit_l2_transp(jj, p, tv2_tok):
        last = None
        proj = projs[jj % 2]
        for qd in range(4):
            bk = 4 + qd
            b.wait("tensor", tmax([tv2_tok], [bankfree[bk]]))
            for w in range(4):
                dc = qd * 4 + w
                tk2 = b.op("tensor", (lambda bk, w, dc, proj: lambda e: e.transpose(
                    ps[bk][:, w * 128:(w + 1) * 128], proj[:, dc * 128:(dc + 1) * 128], idf_sb))(bk, w, dc, proj),
                    s_pe if w == 3 else None, 1 if w == 3 else 0)
            last = tk2
            b.wait("vector", tmax([tk2], [l2_state["free"]] if qd == 0 and jj % 4 == 0 else None))
            dst = l2st.rearrange("p (hc t) -> p hc t", t=512)[:, qd * 4:(qd + 1) * 4, (jj % 4) * 128:((jj % 4) + 1) * 128]
            srcv = ps[bk].rearrange("p (w t) -> p w t", t=128)
            te2 = b.op("vector", (lambda dst, srcv: lambda e: e.tensor_copy(dst, srcv))(dst, srcv), s_dve, 1)
            bankfree[bk] = te2
        if jj % 4 == 3:
            b.wait("sync", te2)
            t0 = p * 1024 + (jj // 4) * 512
            fl = None
            for hc in range(NDC):
                fl = b.op("sync", (lambda hc, t0: lambda e: e.dma_start(
                    l2_sc[hc][:, t0:t0 + 512], l2st[:, hc * 512:(hc + 1) * 512]))(hc, t0), m_l2st, 16)
            l2_state["free"] = fl
        return last

    for p in range(4):
        b.wait("sync", tmax([afm_free], tok_p2_sync))
        tok_afm = None
        for hc in range(NDC):
            tok_afm = b.op("sync", (lambda hc, p: lambda e: e.dma_start(
                afm[:, hc * 1024:(hc + 1) * 1024], af_sc[hc][:, p * 1024:(p + 1) * 1024]))(hc, p), m_afm, 16)
        mm_last = None
        pend = None
        for j in range(8):
            tj = p * 8 + j
            xb = j % 2
            proj = projs[j % 2]
            b.wait("sync", xt_free3[xb])
            tlx = b.op("sync", (lambda tj, xb: lambda e: e.dma_start(xt[xb], x_h[tj * 128:(tj + 1) * 128, :]))(tj, xb), m_x[xb], 16)
            for o in range(4):
                bk = (j * 4 + o) % 4
                b.wait("tensor", tmax([tok_afm], [tok_wo], [bankfree[bk]]))
                for dc in range(NDC):
                    tk = b.op("tensor", (lambda dc, j, o, bk: lambda e: e.matmul(
                        ps[bk], afm[:, dc * 1024 + j * 128: dc * 1024 + (j + 1) * 128],
                        wo_sb[:, dc * 2048 + o * 512: dc * 2048 + (o + 1) * 512],
                        start=(dc == 0), stop=(dc == NDC - 1)))(dc, j, o, bk),
                        s_pe if dc == NDC - 1 else None, 1 if dc == NDC - 1 else 0)
                mm_last = tk
                b.wait("vector", tmax([tk], [t_bo], [proj_free[j % 2]] if o == 0 else None,
                                      [yt_free[1 - xb]] if o == 0 else None))
                tvp = b.op("vector", (lambda o, bk, proj: lambda e: e.scalar_tensor_tensor(
                    proj[:, o * 512:(o + 1) * 512], ps[bk], 1.0, bobc_sb[:, o * 512:(o + 1) * 512],
                    ALU.mult, ALU.add, accum_out=st_sb[:, 12 + o:13 + o]))(o, bk, proj), s_dve, 1)
                bankfree[bk] = tvp
            sq_t = pf[:, sqo[1 - xb]:sqo[1 - xb] + D]
            b.wait_own("vector", tvp)
            t_c1 = b.op("vector", lambda e: e.tensor_tensor(st_sb[:, 12:13], st_sb[:, 12:13], st_sb[:, 13:14], ALU.add), s_dve, 1)
            t_c2 = b.op("vector", lambda e: e.tensor_tensor(st_sb[:, 14:15], st_sb[:, 14:15], st_sb[:, 15:16], ALU.add), s_dve, 1)
            t_sq = b.op("vector", (lambda proj, sq_t: lambda e: e.scalar_tensor_tensor(
                sq_t, proj, 1.0, proj, ALU.mult, ALU.mult, accum_out=st_sb[:, 5:6]))(proj, sq_t), s_dve, 1)
            b.wait_own("vector", t_sq)
            t_c3 = b.op("vector", lambda e: e.tensor_tensor(st_sb[:, 4:5], st_sb[:, 12:13], st_sb[:, 14:15], ALU.add), s_dve, 1)
            t_v1 = b.op("vector", lambda e: e.tensor_scalar(st_sb[:, 5:6], st_sb[:, 5:6], 1.0 / D, EPS, ALU.mult, ALU.add), s_dve, 1)
            b.wait_own("vector", t_c3)
            t_m = b.op("vector", lambda e: e.tensor_scalar(st_sb[:, 4:5], st_sb[:, 4:5], 1.0 / D, 0.0, ALU.mult, ALU.add), s_dve, 1)
            b.wait_own("vector", t_m)
            t_m2 = b.op("vector", lambda e: e.tensor_tensor(st_sb[:, 6:7], st_sb[:, 4:5], st_sb[:, 4:5], ALU.mult), s_dve, 1)
            b.op("vector", (lambda proj: lambda e: e.tensor_scalar(proj, proj, st_sb[:, 4:5], 0.0, ALU.subtract, ALU.add))(proj))
            b.wait_own("vector", t_m2)
            t_var = b.op("vector", lambda e: e.tensor_tensor(st_sb[:, 5:6], st_sb[:, 5:6], st_sb[:, 6:7], ALU.subtract), s_dve, 1)
            b.wait("scalar", t_var)
            t_l1 = b.op("scalar", lambda e: e.activation(st_sb[:, 6:7], st_sb[:, 5:6], AF.Ln, bias=0.0, scale=np.float32(1.0)), s_act, 1)
            b.wait_own("scalar", t_l1)
            ta = b.op("scalar", lambda e: e.activation(st_sb[:, 7:8], st_sb[:, 6:7], AF.Exp, bias=0.0, scale=np.float32(-0.5)), s_act, 1)
            b.wait("vector", tmax([ta], [tlx], [yt_free[xb]]))
            ty = b.op("vector", (lambda xb, proj: lambda e: e.scalar_tensor_tensor(
                yt[xb], proj, st_sb[:, 7:8], xt[xb], ALU.mult, ALU.add,
                accum_out=st_sb[:, 8:9]))(xb, proj), s_dve, 1)
            xt_free3[xb] = ty
            b.wait("sync", ty)
            tsy = b.op("sync", (lambda tj, xb: lambda e: e.dma_start(
                out_h[tj * 128:(tj + 1) * 128, :], yt[xb]))(tj, xb), m_y[xb], 16)
            yt_free[xb] = tsy
            t_sq2 = b.op("vector", (lambda xb, sq_t: lambda e: e.scalar_tensor_tensor(
                sq_t, yt[xb], 1.0, yt[xb], ALU.mult, ALU.mult, accum_out=st_sb[:, 9:10]))(xb, sq_t), s_dve, 1)
            b.wait_own("vector", t_sq2)
            t_n1 = b.op("vector", lambda e: e.tensor_scalar(st_sb[:, 8:9], st_sb[:, 8:9], 1.0 / D, 0.0, ALU.mult, ALU.add), s_dve, 1)
            t_n2 = b.op("vector", lambda e: e.tensor_scalar(st_sb[:, 9:10], st_sb[:, 9:10], 1.0 / D, EPS, ALU.mult, ALU.add), s_dve, 1)
            b.wait_own("vector", t_n2)
            t_n3 = b.op("vector", lambda e: e.tensor_tensor(st_sb[:, 10:11], st_sb[:, 8:9], st_sb[:, 8:9], ALU.mult), s_dve, 1)
            b.wait_own("vector", t_n3)
            t_n4 = b.op("vector", lambda e: e.tensor_tensor(st_sb[:, 9:10], st_sb[:, 9:10], st_sb[:, 10:11], ALU.subtract), s_dve, 1)
            b.wait("scalar", t_n4)
            t_l2 = b.op("scalar", lambda e: e.activation(st_sb[:, 10:11], st_sb[:, 9:10], AF.Ln, bias=0.0, scale=np.float32(1.0)), s_act, 1)
            b.wait_own("scalar", t_l2)
            t_e2 = b.op("scalar", lambda e: e.activation(st_sb[:, 11:12], st_sb[:, 10:11], AF.Exp, bias=0.0, scale=np.float32(-0.5)), s_act, 1)
            b.wait("vector", t_e2)
            t_b = b.op("vector", lambda e: e.scalar_tensor_tensor(
                st_sb[:, 10:11], st_sb[:, 8:9], -1.0, st_sb[:, 11:12], ALU.mult, ALU.mult), s_dve, 1)
            b.wait_own("vector", t_b)
            tv2 = b.op("vector", (lambda xb, proj: lambda e: e.tensor_scalar(
                proj, yt[xb], st_sb[:, 11:12], st_sb[:, 10:11], ALU.mult, ALU.add))(xb, proj), s_dve, 1)
            if pend is not None:
                proj_free[pend[0] % 2] = emit_l2_transp(pend[0], p, pend[1])
            pend = (j, tv2)
        proj_free[pend[0] % 2] = emit_l2_transp(pend[0], p, pend[1])
        afm_free = mm_last
    l2st_free = l2_state["free"]

    tok_p3_ysync = tmax([b.final(s) for s in m_y])
    tok_p3_pe = (s_pe, b.counts["s_pe"])
    tok_p3_dve = (s_dve, b.counts["s_dve"])
    tok_p3_act = (s_act, b.counts["s_act"])

    if PHASES < 4:
        return _finish(nc, b)

    hdn = pbf[:, 0:45056]
    ln2fm = pbf[:, 45056:61440]
    gwp = [pbf[:, 61440:63488], pbf[:, 63488:65536]]
    uwp = [pbf[:, 65536:67584], pbf[:, 67584:69632]]
    dwp = [pbf[:, 69632 + i * 512: 69632 + (i + 1) * 512] for i in range(4)]
    silu_sb = [pf[:, i * 512:(i + 1) * 512] for i in range(4)]
    dsbc_sb = pf[:, 2048:4096]
    ffnst = [pf[:, 4096 + i * 512: 4096 + (i + 1) * 512] for i in range(4)]

    b.wait("sync", tmax([tok_p3_pe], [tok_p3_dve]))
    t_ds = b.op("sync", lambda e: e.dma_start(dsbc_sb, dsbc_h), m_const, 16)

    gwp_free = [tok_p3_pe, tok_p3_pe]
    uwp_free = [tok_p3_pe, tok_p3_pe]
    dwp_free = [None, None, None, None]
    ln2fm_free = tok_p3_pe
    ffnst_free = [tok_p3_pe] * 4
    hdn_free = None

    ft = [pf[:, 6144:8192], pf[:, 8192:10240]]
    ft_free = [tok_p3_dve, tok_p3_dve]
    ffn_done_tok = [None]

    def emit_epilogue_tile(pp, j):
        g = pp * 8 + j
        fb = j % 2
        b.wait("sync", tmax([ft_free[fb]], [b.final(s) for s in m_ffn] if j % 2 == 0 else None,
                            tok_p3_ysync if pp == 0 and j == 0 else None))
        tf = b.op("sync", (lambda g, fb: lambda e: e.dma_start(ft[fb], ffn_sc[g * 128:(g + 1) * 128, :]))(g, fb), m_ft[fb], 16)
        b.wait("vector", tf)
        ta5 = emit_ln(12, ft[fb], 10240)
        b.wait("vector", ta5)
        tapp = b.op("vector", (lambda fb: lambda e: e.tensor_scalar(ft[fb], ft[fb], st_sb[:, 15:16], 0.0, ALU.mult, ALU.add))(fb), s_dve, 1)
        b.wait("gpsimd", tmax([tapp], tok_p3_ysync if pp == 0 and j == 0 else None))
        tacc = b.op("gpsimd", (lambda g, fb: lambda e: e.dma_start(
            out_h[g * 128:(g + 1) * 128, :], ft[fb], accum_op=ALU.add))(g, fb), m_out[fb], 16)
        ft_free[fb] = tacc

    for p in range(4):
        b.wait("sync", tmax([ln2fm_free], [l2st_free]))
        tok_l2 = None
        for hc in range(NDC):
            tok_l2 = b.op("sync", (lambda hc, p: lambda e: e.dma_start(
                ln2fm[:, hc * 1024:(hc + 1) * 1024], l2_sc[hc][:, p * 1024:(p + 1) * 1024]))(hc, p), m_l2f, 16)
        hdn_last = None
        gu_last = None
        for ip in range(NIC):
            wb = ip % 2
            b.wait("gpsimd", gwp_free[wb])
            tg = b.op("gpsimd", (lambda ip, wb: lambda e: e.dma_start(
                gwp[wb].rearrange("p (hc i) -> p hc i", i=128), gw_h[ip]))(ip, wb), m_gw[wb], 16)
            b.wait("gpsimd", uwp_free[wb])
            tu = b.op("gpsimd", (lambda ip, wb: lambda e: e.dma_start(
                uwp[wb].rearrange("p (hc i) -> p hc i", i=128), uw_h[ip]))(ip, wb), m_uw[wb], 16)
            bbase = (ip % 2) * 4
            b.wait("tensor", tmax([tg], [tu], [tok_l2], [hdn_free] if ip == 0 else None,
                                  [bankfree[bbase]], [bankfree[bbase + 1]],
                                  [bankfree[bbase + 2]], [bankfree[bbase + 3]]))
            tk = None
            for hc in range(NDC):
                stt_ = hc == 0
                sp = hc == NDC - 1
                for k in range(2):
                    b.op("tensor", (lambda wb, hc, k, bbase, stt_, sp: lambda e: e.matmul(
                        ps[bbase + k], gwp[wb][:, hc * 128:(hc + 1) * 128],
                        ln2fm[:, hc * 1024 + k * 512: hc * 1024 + (k + 1) * 512],
                        start=stt_, stop=sp))(wb, hc, k, bbase, stt_, sp))
                for k in range(2):
                    last = sp and k == 1
                    tk = b.op("tensor", (lambda wb, hc, k, bbase, stt_, sp: lambda e: e.matmul(
                        ps[bbase + 2 + k], uwp[wb][:, hc * 128:(hc + 1) * 128],
                        ln2fm[:, hc * 1024 + k * 512: hc * 1024 + (k + 1) * 512],
                        start=stt_, stop=sp))(wb, hc, k, bbase, stt_, sp),
                        s_pe if last else None, 1 if last else 0)
            gu_last = tk
            gwp_free[wb] = tk
            uwp_free[wb] = tk
            for k in range(2):
                sb4 = (ip % 2) * 2 + k
                b.wait("scalar", tk)
                tsl = b.op("scalar", (lambda sb4, bbase, k, ip: lambda e: e.activation(
                    silu_sb[sb4], ps[bbase + k], AF.Silu, scale=gs_sb[:, ip:ip + 1]))(sb4, bbase, k, ip), s_act, 1)
                bankfree[bbase + k] = tsl
                b.wait("vector", tsl)
                th = b.op("vector", (lambda sb4, bbase, k, ip: lambda e: e.scalar_tensor_tensor(
                    hdn[:, ip * 1024 + k * 512: ip * 1024 + (k + 1) * 512],
                    ps[bbase + 2 + k], us_sb[:, ip:ip + 1], silu_sb[sb4], ALU.mult, ALU.mult))(sb4, bbase, k, ip), s_dve, 1)
                bankfree[bbase + 2 + k] = th
                hdn_last = th
        tok_hdn = hdn_last
        ln2fm_free = gu_last
        if DBG and p == 0:
            b.wait("sync", tok_hdn)
            b.op("sync", lambda e: e.dma_start(hdn_dbg, hdn), m_dbg, 16)
        down_last = None
        for o in range(4):
            for ip in range(NIC):
                wb = ip % 4
                b.wait("sync", tmax([dwp_free[wb]], tok_dwprep if p == 0 and o == 0 and ip < 4 else None))
                td = b.op("sync", (lambda ip, o, wb: lambda e: e.dma_start(
                    dwp[wb], dwbf_sc[ip][:, o * 512:(o + 1) * 512]))(ip, o, wb), m_dwp[wb], 16)
                if ip == 0:
                    b.wait("tensor", tmax([tok_hdn], *[[bankfree[i]] for i in range(8)]))
                b.wait("tensor", td)
                for j in range(8):
                    last = (j == 7)
                    tk = b.op("tensor", (lambda wb, ip, j: lambda e: e.matmul(
                        ps[j], hdn[:, ip * 1024 + j * 128: ip * 1024 + (j + 1) * 128],
                        dwp[wb], start=(ip == 0), stop=(ip == NIC - 1)))(wb, ip, j),
                        s_pe if last else None, 1 if last else 0)
                dwp_free[wb] = tk
            down_last = tk
            for j in range(8):
                fb = j % 4
                b.wait("vector", tmax([down_last], [t_ds], [ffnst_free[fb]]))
                tvd = b.op("vector", (lambda j, o, fb: lambda e: e.tensor_tensor(
                    ffnst[fb], ps[j], dsbc_sb[:, o * 512:(o + 1) * 512], ALU.mult))(j, o, fb), s_dve, 1)
                bankfree[j] = tvd
                b.wait("sync", tvd)
                tj = p * 8 + j
                tsf = b.op("sync", (lambda tj, o, fb: lambda e: e.dma_start(
                    ffn_sc[tj * 128:(tj + 1) * 128, o * 512:(o + 1) * 512], ffnst[fb]))(tj, o, fb), m_ffn[fb], 16)
                ffnst_free[fb] = tsf
            if p > 0:
                emit_epilogue_tile(p - 1, 2 * o)
                emit_epilogue_tile(p - 1, 2 * o + 1)
        hdn_free = down_last
        ffn_done_tok[0] = tmax([b.final(s) for s in m_ffn])

    for j in range(8):
        emit_epilogue_tile(3, j)
    tok_p4_sync = tmax([b.final(s) for s in m_ffn])

    return _finish(nc, b)

def _finish(nc, b):
    for name in b.semnames:
        if b.own[name] is None and b.counts[name] > 0:
            b.q["sync"].append(("w", name, b.counts[name]))
    stack = contextlib.ExitStack()
    b.emit(nc, stack)
    stack.close()
    return nc


def stage_inputs(L0_attn_Wqkv, L0_attn_bqkv, L0_attn_Wo, L0_attn_bo,
                 L0_ffn_gate_w, L0_ffn_up_w, L0_ffn_down_w,
                 L0_ffn_gate_s, L0_ffn_up_s, L0_ffn_down_s):
    f32 = lambda a: np.ascontiguousarray(np.asarray(a, dtype=np.float32))
    Wqkv = f32(L0_attn_Wqkv)
    sh = {}
    sh["wqkv_sw"] = np.ascontiguousarray(Wqkv.reshape(NDC, 128, 12, 512).transpose(2, 1, 0, 3))
    bq = f32(L0_attn_bqkv)
    sh["bqkv_t"] = np.ascontiguousarray(bq.reshape(48, 128).T)
    sh["bv_bc"] = np.ascontiguousarray(np.broadcast_to(bq[4096:6144], (128, D)))
    sh["wo_n"] = f32(L0_attn_Wo)
    sh["bo_bc"] = np.ascontiguousarray(np.broadcast_to(f32(L0_attn_bo), (128, D)))
    sh["gw_sw"] = np.ascontiguousarray(f32(L0_ffn_gate_w).reshape(NIC, 128, NDC, 128).transpose(0, 3, 2, 1))
    sh["uw_sw"] = np.ascontiguousarray(f32(L0_ffn_up_w).reshape(NIC, 128, NDC, 128).transpose(0, 3, 2, 1))
    sh["dw_t"] = np.ascontiguousarray(f32(L0_ffn_down_w).T.reshape(NIC, 128, D))
    sh["gs_t"] = np.ascontiguousarray(f32(L0_ffn_gate_s).reshape(NIC, 128).T)
    sh["us_t"] = np.ascontiguousarray(f32(L0_ffn_up_s).reshape(NIC, 128).T)
    sh["ds_bc"] = np.ascontiguousarray(np.broadcast_to(f32(L0_ffn_down_s), (128, D)))
    sl, tl = np.arange(128)[:, None], np.arange(512)[None, :]
    masks = np.stack([(tl >= sl + 128 * k) for k in range(4)], 0).astype(ml_dtypes.bfloat16)
    sh["masks"] = np.ascontiguousarray(masks.transpose(1, 0, 2).reshape(128, 2048))
    sh["ident_f32"] = np.eye(128, dtype=np.float32)
    sh["ones_f32"] = np.ones((1, 128), dtype=np.float32)
    return sh


def kernel(L0_attn_x, L0_attn_Wqkv, L0_attn_bqkv, L0_attn_Wo, L0_attn_bo,
           L0_ffn_gate_w, L0_ffn_up_w, L0_ffn_down_w,
           L0_ffn_gate_s, L0_ffn_up_s, L0_ffn_down_s):
    x = np.ascontiguousarray(np.asarray(L0_attn_x, dtype=np.float32))
    sh = stage_inputs(L0_attn_Wqkv, L0_attn_bqkv, L0_attn_Wo, L0_attn_bo,
                      L0_ffn_gate_w, L0_ffn_up_w, L0_ffn_down_w,
                      L0_ffn_gate_s, L0_ffn_up_s, L0_ffn_down_s)
    nc = build_nc()
    in_maps = [dict(sh, x=np.ascontiguousarray(x[c])) for c in range(8)]
    res = run_bass_kernel_spmd(nc, in_maps, core_ids=list(range(8)))
    return np.stack([r["out"] for r in res.results], axis=0)


# version 18 (final)
# speedup vs baseline: 1.1077x; 1.0811x over previous
import contextlib
import numpy as np
import ml_dtypes
import concourse.bass as bass
import concourse.mybir as mybir
from concourse.bass_utils import run_bass_kernel_spmd

F32 = mybir.dt.float32
BF16 = mybir.dt.bfloat16
AF = mybir.ActivationFunctionType
AX = mybir.AxisListType
ALU = mybir.AluOpType

S = 4096
D = 2048
NDC = 16
NIC = 44
NH = 16
SCALE = np.float32(1.0 / np.sqrt(128.0))
ESHIFT = np.float32(-8.0)
EPS = 1e-5

ENGS = ("sync", "gpsimd", "tensor", "scalar", "vector")
PHASES = 5
DBG = False


def tmax(*toks):
    out = {}

    def add(t):
        if t is None:
            return
        if isinstance(t, list):
            for u in t:
                add(u)
        else:
            s, v = t
            if v > out.get(s, -1):
                out[s] = v
    for t in toks:
        add(t)
    return [(s, v) for s, v in out.items()] or None


class Builder:
    def __init__(self):
        self.q = {e: [] for e in ENGS}
        self.semnames = []
        self.counts = {}
        self.waited = {e: {} for e in ENGS}
        self.own = {}

    def sem(self, name, owner):
        self.semnames.append(name)
        self.counts[name] = 0
        self.own[name] = owner
        return name

    def wait(self, eng, tok):
        if tok is None:
            return
        if isinstance(tok, list):
            for t in tmax(tok):
                self._wait1(eng, t)
            return
        self._wait1(eng, tok)

    def _wait1(self, eng, tok):
        s, v = tok
        if v <= 0:
            return
        if self.own[s] == eng:
            return
        if self.waited[eng].get(s, -1) >= v:
            return
        self.waited[eng][s] = v
        self.q[eng].append(("w", s, v))

    def wait_own(self, eng, tok):
        s, v = tok
        if v <= 0 or self.waited[eng].get(s, -1) >= v:
            return
        self.waited[eng][s] = v
        self.q[eng].append(("w", s, v))

    def op(self, eng, fn, sem=None, inc=0):
        self.q[eng].append(("o", fn, sem, inc))
        if sem is not None:
            self.counts[sem] += inc
            return (sem, self.counts[sem])
        return None

    def final(self, sem):
        return (sem, self.counts[sem])

    def emit(self, nc, stack):
        hs = {n: stack.enter_context(nc.semaphore(n)) for n in self.semnames}
        with nc.Block() as block:
            for eng in ENGS:
                items = self.q[eng]

                def make(items):
                    def body(e):
                        for it in items:
                            if it[0] == "w":
                                e.wait_ge(hs[it[1]], it[2])
                            else:
                                ins = it[1](e)
                                if it[2] is not None:
                                    ins.then_inc(hs[it[2]], it[3])
                    return body

                getattr(block, eng)(make(items))


def build_nc():
    nc = bass.Bass("TRN2", target_bir_lowering=False)
    b = Builder()
    s_pe = b.sem("s_pe", "tensor")
    s_act = b.sem("s_act", "scalar")
    s_dve = b.sem("s_dve", "vector")
    s_gp = b.sem("s_gp", "gpsimd")
    ds = lambda name, n: [b.sem("%s%d" % (name, i), None) for i in range(n)]
    m_const = b.sem("m_const", None)
    m_x = ds("m_x", 2)
    m_wq = ds("m_wq", 2)
    m_qkst = ds("m_qkst", 2)
    m_vst = ds("m_vst", 2)
    m_head = ds("m_head", 2)
    m_af = ds("m_af", 2)
    m_dwl = ds("m_dwl", 2)
    m_dws = ds("m_dws", 2)
    m_afm = b.sem("m_afm", None)
    m_wo = b.sem("m_wo", None)
    m_y = ds("m_y", 2)
    m_l2st = b.sem("m_l2st", None)
    m_l2f = b.sem("m_l2f", None)
    m_gw = ds("m_gw", 2)
    m_uw = ds("m_uw", 2)
    m_dwp = ds("m_dwp", 4)
    m_ffn = ds("m_ffn", 4)
    m_ft = ds("m_ft", 2)
    m_y5 = ds("m_y5", 2)
    m_out = ds("m_out", 2)
    m_t1 = b.sem("m_t1", None)
    m_t2 = b.sem("m_t2", None)
    m_dbg = b.sem("m_dbg", None)

    x_h = nc.dram_tensor("x", [S, D], F32, kind="ExternalInput").ap()
    wqkv_h = nc.dram_tensor("wqkv_sw", [12, 128, NDC, 512], F32, kind="ExternalInput").ap()
    bqkv_h = nc.dram_tensor("bqkv_t", [128, 48], F32, kind="ExternalInput").ap()
    bvbc_h = nc.dram_tensor("bv_bc", [128, D], F32, kind="ExternalInput").ap()
    wo_h = nc.dram_tensor("wo_n", [D, D], F32, kind="ExternalInput").ap()
    bobc_h = nc.dram_tensor("bo_bc", [128, D], F32, kind="ExternalInput").ap()
    gw_h = nc.dram_tensor("gw_sw", [NIC, 128, NDC, 128], F32, kind="ExternalInput").ap()
    uw_h = nc.dram_tensor("uw_sw", [NIC, 128, NDC, 128], F32, kind="ExternalInput").ap()
    dw_h = nc.dram_tensor("dw_t", [NIC, 128, D], F32, kind="ExternalInput").ap()
    gs_h = nc.dram_tensor("gs_t", [128, NIC], F32, kind="ExternalInput").ap()
    us_h = nc.dram_tensor("us_t", [128, NIC], F32, kind="ExternalInput").ap()
    dsbc_h = nc.dram_tensor("ds_bc", [128, D], F32, kind="ExternalInput").ap()
    mask_h = nc.dram_tensor("masks", [128, 2048], BF16, kind="ExternalInput").ap()
    idf_h = nc.dram_tensor("ident_f32", [128, 128], F32, kind="ExternalInput").ap()
    onef_h = nc.dram_tensor("ones_f32", [1, 128], F32, kind="ExternalInput").ap()
    out_h = nc.dram_tensor("out", [S, D], F32, kind="ExternalOutput").ap()

    KI = "ExternalOutput" if DBG else "Internal"
    q_sc = nc.dram_tensor("q_sc", [NH, 128, S], BF16, kind=KI).ap()
    k_sc = nc.dram_tensor("k_sc", [NH, 128, S], BF16, kind=KI).ap()
    v_sc = nc.dram_tensor("v_sc", [NH, 32, 128, 128], BF16, kind=KI).ap()
    af_sc = nc.dram_tensor("af_sc", [NDC, 128, S], BF16, kind=KI).ap()
    l2_sc = nc.dram_tensor("l2_sc", [NDC, 128, S], BF16, kind=KI).ap()
    dwbf_sc = nc.dram_tensor("dwbf_sc", [NIC, 128, D], BF16, kind=KI).ap()
    ffn_sc = nc.dram_tensor("ffn_sc", [S, D], F32, kind=KI).ap()
    hdn_dbg = nc.dram_tensor("hdn_dbg", [128, 45056], BF16, kind=KI).ap() if DBG else None

    pbf = nc.alloc_sbuf_tensor("pool_bf", [128, 71680], BF16).ap()
    pf = nc.alloc_sbuf_tensor("pool_f32", [128, 14336], F32).ap()
    mask_sb = nc.alloc_sbuf_tensor("mask_sb", [128, 2048], BF16).ap()
    idf_sb = nc.alloc_sbuf_tensor("idf_sb", [128, 128], F32).ap()
    onef_sb = nc.alloc_sbuf_tensor("onef_sb", [1, 128], F32).ap()
    bqkv_sb = nc.alloc_sbuf_tensor("bqkv_sb", [128, 48], F32).ap()
    gs_sb = nc.alloc_sbuf_tensor("gs_sb", [128, NIC], F32).ap()
    us_sb = nc.alloc_sbuf_tensor("us_sb", [128, NIC], F32).ap()
    st_sb = nc.alloc_sbuf_tensor("st_sb", [128, 16], F32).ap()
    iz_sb = nc.alloc_sbuf_tensor("iz_sb", [1, 1024], F32).ap()
    zbc_sb = nc.alloc_sbuf_tensor("zbc_sb", [128, 512], F32).ap()

    ps = [nc.alloc_psum_tensor("ps%d" % i, [128, 512], F32).ap() for i in range(8)]
    bankfree = [None] * 8

    b.op("sync", lambda e: e.dma_start(mask_sb[:], mask_h), m_const, 16)
    b.op("sync", lambda e: e.dma_start(idf_sb[:], idf_h), m_const, 16)
    b.op("sync", lambda e: e.dma_start(onef_sb[:], onef_h), m_const, 16)
    b.op("sync", lambda e: e.dma_start(bqkv_sb[:], bqkv_h), m_const, 16)
    b.op("sync", lambda e: e.dma_start(gs_sb[:], gs_h), m_const, 16)
    tok_const = b.op("sync", lambda e: e.dma_start(us_sb[:], us_h), m_const, 16)

    def emit_ln(c, tile, sq_off):
        b.op("vector", (lambda c, tile: lambda e: e.reduce_sum(st_sb[:, c:c + 1], tile, axis=AX.X))(c, tile), s_dve, 1)
        t2 = b.op("vector", (lambda c, tile, sq_off: lambda e: e.scalar_tensor_tensor(
            pf[:, sq_off:sq_off + D], tile, 1.0, tile, ALU.mult, ALU.mult,
            accum_out=st_sb[:, c + 1:c + 2]))(c, tile, sq_off), s_dve, 1)
        b.wait_own("vector", t2)
        b.op("vector", (lambda c: lambda e: e.tensor_scalar(st_sb[:, c:c + 1], st_sb[:, c:c + 1], 1.0 / D, 0.0, ALU.mult, ALU.add))(c), s_dve, 1)
        t4 = b.op("vector", (lambda c: lambda e: e.tensor_scalar(st_sb[:, c + 1:c + 2], st_sb[:, c + 1:c + 2], 1.0 / D, EPS, ALU.mult, ALU.add))(c), s_dve, 1)
        b.wait_own("vector", t4)
        t5 = b.op("vector", (lambda c: lambda e: e.tensor_tensor(st_sb[:, c + 2:c + 3], st_sb[:, c:c + 1], st_sb[:, c:c + 1], ALU.mult))(c), s_dve, 1)
        b.op("vector", (lambda c, tile: lambda e: e.tensor_scalar(tile, tile, st_sb[:, c:c + 1], 0.0, ALU.subtract, ALU.add))(c, tile))
        b.wait_own("vector", t5)
        t7 = b.op("vector", (lambda c: lambda e: e.tensor_tensor(st_sb[:, c + 1:c + 2], st_sb[:, c + 1:c + 2], st_sb[:, c + 2:c + 3], ALU.subtract))(c), s_dve, 1)
        b.wait("scalar", t7)
        t8 = b.op("scalar", (lambda c: lambda e: e.activation(st_sb[:, c + 2:c + 3], st_sb[:, c + 1:c + 2], AF.Ln, bias=0.0, scale=np.float32(1.0)))(c), s_act, 1)
        b.wait_own("scalar", t8)
        ta = b.op("scalar", (lambda c: lambda e: e.activation(st_sb[:, c + 3:c + 4], st_sb[:, c + 2:c + 3], AF.Exp, bias=0.0, scale=np.float32(-0.5)))(c), s_act, 1)
        return ta

    ln1fm = [pbf[:, 0:16384], pbf[:, 16384:32768]]
    wqp = [pbf[:, 32768:40960], pbf[:, 40960:49152]]
    qkst = [pbf[:, 49152:51200], pbf[:, 51200:53248]]
    vst = [pbf[:, 53248:53760], pbf[:, 53760:54272]]
    xt = [pf[:, 0:2048], pf[:, 2048:4096]]
    ln1f = pf[:, 4096:6144]
    bvbc_sb = pf[:, 8192:10240]

    t_bv = b.op("sync", lambda e: e.dma_start(bvbc_sb, bvbc_h), m_const, 16)
    b.wait("tensor", t_bv)
    b.wait("vector", t_bv)
    b.wait("scalar", t_bv)

    xt_free = [None, None]
    wqp_free = [None, None]
    qkst_free = [None, None]
    vst_free = [None, None]
    ln1fm_ready = [[None, None], [None, None]]
    ln1fm_free = [None, None]
    ln1f_free = None

    for p in range(4):
        par = p % 2
        last_evac = None
        for j in range(8):
            tj = p * 8 + j
            xb = j % 2
            b.wait("sync", xt_free[xb])
            tl = b.op("sync", (lambda tj, xb: lambda e: e.dma_start(xt[xb], x_h[tj * 128:(tj + 1) * 128, :]))(tj, xb), m_x[xb], 16)
            b.wait("vector", tmax([tl]))
            ta = emit_ln(0, xt[xb], 6144)
            b.wait("vector", tmax([ta], [ln1f_free]))
            tv = b.op("vector", (lambda xb: lambda e: e.tensor_scalar(ln1f, xt[xb], st_sb[:, 3:4], 0.0, ALU.mult, ALU.add))(xb), s_dve, 1)
            xt_free[xb] = tv
            for qd in range(4):
                bk = 4 + qd
                b.wait("tensor", tmax([tv], [bankfree[bk]], [ln1fm_free[par]] if j == 0 and qd == 0 else None))
                for w in range(4):
                    dc = qd * 4 + w
                    tk = b.op("tensor", (lambda bk, w, dc: lambda e: e.transpose(
                        ps[bk][:, w * 128:(w + 1) * 128], ln1f[:, dc * 128:(dc + 1) * 128], idf_sb))(bk, w, dc),
                        s_pe if w == 3 else None, 1 if w == 3 else 0)
                ln1f_free = tk
                b.wait("vector", tmax([tk]))
                dst = ln1fm[par].rearrange("p (dc t) -> p dc t", t=1024)[:, qd * 4:(qd + 1) * 4, j * 128:(j + 1) * 128]
                srcv = ps[bk].rearrange("p (w t) -> p w t", t=128)
                te = b.op("vector", (lambda dst, srcv: lambda e: e.tensor_copy(dst, srcv))(dst, srcv), s_dve, 1)
                bankfree[bk] = te
                last_evac = te
            if j == 3:
                ln1fm_ready[par][0] = last_evac
        ln1fm_ready[par][1] = last_evac

        mmtok = None
        for pn in range(12):
            wb = pn % 2
            b.wait("gpsimd", tmax([wqp_free[wb]], [tok_const] if p == 0 and pn < 2 else None))
            tw = b.op("gpsimd", (lambda pn, wb: lambda e: e.dma_start(
                wqp[wb].rearrange("p (dc o) -> p dc o", o=512), wqkv_h[pn]))(pn, wb), m_wq[wb], 16)
            if pn < 8:
                for k in range(2):
                    sb = (pn * 2 + k) % 2
                    evtoks = []
                    for oc in range(4):
                        bk = (pn * 8 + k * 4 + oc) % 4
                        b.wait("tensor", tmax([tw], [ln1fm_ready[par][k]], [bankfree[bk]]))
                        for dc in range(NDC):
                            tk = b.op("tensor", (lambda wb, dc, oc, bk, par, k: lambda e: e.matmul(
                                ps[bk], wqp[wb][:, dc * 512 + oc * 128: dc * 512 + (oc + 1) * 128],
                                ln1fm[par][:, dc * 1024 + k * 512: dc * 1024 + (k + 1) * 512],
                                start=(dc == 0), stop=(dc == NDC - 1)))(wb, dc, oc, bk, par, k),
                                s_pe if dc == NDC - 1 else None, 1 if dc == NDC - 1 else 0)
                        mmtok = tk
                        cglob = pn * 4 + oc
                        b.wait("scalar", tmax([tk], [qkst_free[sb]] if oc == 0 else None))
                        tac = b.op("scalar", (lambda sb, oc, bk, cglob: lambda e: e.activation(
                            qkst[sb][:, oc * 512:(oc + 1) * 512], ps[bk], AF.Identity,
                            bias=bqkv_sb[:, cglob:cglob + 1]))(sb, oc, bk, cglob), s_act, 1)
                        bankfree[bk] = tac
                        evtoks.append(tac)
                    b.wait("sync", evtoks)
                    dest = q_sc if pn < 4 else k_sc
                    h0 = (pn * 4) % 16
                    t0 = p * 1024 + k * 512
                    tst = b.op("sync", (lambda dest, h0, t0, sb: lambda e: e.dma_start(
                        dest[h0:h0 + 4, :, t0:t0 + 512].rearrange("h p t -> p h t"),
                        qkst[sb].rearrange("p (h t) -> p h t", h=4)))(dest, h0, t0, sb), m_qkst[sb], 16)
                    qkst_free[sb] = tst
            else:
                for j in range(8):
                    tj = p * 8 + j
                    sb = (pn * 8 + j) % 2
                    bk = (pn * 8 + j) % 4
                    b.wait("tensor", tmax([tw], [ln1fm_ready[par][j // 4]], [bankfree[bk]]))
                    for dc in range(NDC):
                        tk = b.op("tensor", (lambda wb, dc, j, bk, par: lambda e: e.matmul(
                            ps[bk], ln1fm[par][:, dc * 1024 + j * 128: dc * 1024 + (j + 1) * 128],
                            wqp[wb][:, dc * 512:(dc + 1) * 512],
                            start=(dc == 0), stop=(dc == NDC - 1)))(wb, dc, j, bk, par),
                            s_pe if dc == NDC - 1 else None, 1 if dc == NDC - 1 else 0)
                    mmtok = tk
                    b.wait("vector", tmax([tk], [vst_free[sb]], [t_bv]))
                    tvv = b.op("vector", (lambda sb, bk, pn: lambda e: e.scalar_tensor_tensor(
                        vst[sb], ps[bk], 1.0, bvbc_sb[:, (pn - 8) * 512:(pn - 7) * 512],
                        ALU.mult, ALU.add))(sb, bk, pn), s_dve, 1)
                    bankfree[bk] = tvv
                    b.wait("sync", tvv)
                    h0 = (pn - 8) * 4
                    tsv = b.op("sync", (lambda sb, h0, tj: lambda e: e.dma_start(
                        v_sc[h0:h0 + 4, tj, :, :].rearrange("h p w -> p h w"),
                        vst[sb].rearrange("p (h w) -> p h w", w=128)))(sb, h0, tj), m_vst[sb], 16)
                    vst_free[sb] = tsv
            wqp_free[wb] = mmtok
        ln1fm_free[par] = mmtok

    tok_p1_sync = tmax([b.final(s) for s in m_qkst + m_vst])
    tok_p1_pe = (s_pe, b.counts["s_pe"])

    if PHASES < 2:
        return _finish(nc, b)

    qb = [pbf[:, 0:4096], pbf[:, 4096:8192]]
    kb = [pbf[:, 8192:12288], pbf[:, 12288:16384]]
    pbuf = [pbf[:, 16384:32768], pbf[:, 46336:62720]]
    va_view = [pbf[:, 32768 + k * 4224: 32768 + (k + 1) * 4224].rearrange("p (c w) -> p c w", w=132) for k in range(2)]
    afst = [pbf[:, 41216:41728], pbf[:, 41728:42240]]
    dwc = [pbf[:, 42240:44288], pbf[:, 44288:46336]]

    b.wait("vector", tok_p1_pe)
    tok_ones = []
    for k2 in range(2):
        tok_ones.append(b.op("vector", (lambda k2: lambda e: e.memset(va_view[k2][:, :, 128:129], 1.0))(k2), s_dve, 1))

    pe_head_last = [tok_p1_pe, tok_p1_pe]
    afst_free = [tok_p1_pe, tok_p1_pe]
    pbuf_free = [None, None]
    iz_free = [None, None]
    dwc_store = [None, None]
    dw_ip = [0]
    head_loads = [None, None]

    def emit_head_loads(h):
        hb = h % 2
        b.wait("sync", tmax(tok_p1_sync, [pe_head_last[hb]], tok_ones if h < 2 else None))
        b.op("sync", (lambda h, hb: lambda e: e.dma_start(qb[hb], q_sc[h]))(h, hb), m_head[hb], 16)
        b.op("sync", (lambda h, hb: lambda e: e.dma_start(kb[hb], k_sc[h]))(h, hb), m_head[hb], 16)
        ld = None
        for vq in range(4):
            ld = b.op("sync", (lambda h, hb, vq: lambda e: e.dma_start(
                va_view[hb][:, vq * 8:(vq + 1) * 8, 0:128],
                v_sc[h][vq * 8:(vq + 1) * 8].rearrange("c p w -> p c w")))(h, hb, vq), m_head[hb], 16)
        head_loads[hb] = ld
        for _ in range(3):
            ip = dw_ip[0]
            if ip < NIC:
                cb = ip % 2
                b.wait("gpsimd", tmax([tok_p1_pe] if ip < 2 else None, [dwc_store[cb]]))
                tld = b.op("gpsimd", (lambda ip, cb: lambda e: e.dma_start(dwc[cb], dw_h[ip]))(ip, cb), m_dwl[cb], 16)
                b.wait("sync", tld)
                dwc_store[cb] = b.op("sync", (lambda ip, cb: lambda e: e.dma_start(dwbf_sc[ip], dwc[cb]))(ip, cb), m_dws[cb], 16)
                dw_ip[0] += 1

    def emit_pv_chunk(u, c):
        hb, par, nk = u["hb"], u["par"], u["nk"]
        bo_, bz = 2 + 3 * par, 3 + 3 * par
        b.wait("tensor", tmax([u["exp"][c]],
                              [bankfree[bo_], bankfree[bz]] if c == 0 else None))
        pb = pbuf[par]
        b.op("tensor", (lambda hb, c, st, sp, bo_, pb: lambda e: e.matmul(
            ps[bo_], va_view[hb][:, c, 0:128], pb[:, c * 512:(c + 1) * 512],
            start=st, stop=sp))(hb, c, c == 0, c == nk - 1, bo_, pb))
        tz = b.op("tensor", (lambda hb, c, st, sp, bz, pb: lambda e: e.matmul(
            ps[bz][0:1, :], va_view[hb][:, c, 128:129], pb[:, c * 512:(c + 1) * 512],
            start=st, stop=sp))(hb, c, c == 0, c == nk - 1, bz, pb), s_pe, 1)
        return tz

    def emit_recip(u, pv_last):
        par = u["par"]
        bz = 3 + 3 * par
        b.wait("vector", tmax([pv_last], [iz_free[par]]))
        trc = b.op("vector", (lambda bz, par: lambda e: e.reciprocal(
            iz_sb[0:1, par * 512:(par + 1) * 512], ps[bz][0:1, :]))(bz, par), s_dve, 1)
        bankfree[bz] = trc
        u["trc"] = trc

    def emit_evacs(u):
        par = u["par"]
        bo_, bb_ = 2 + 3 * par, 4 + 3 * par
        b.wait("tensor", tmax([u["trc"]], [bankfree[bb_]]))
        tbc = b.op("tensor", (lambda bb_, par: lambda e: e.matmul(
            ps[bb_], onef_sb, iz_sb[0:1, par * 512:(par + 1) * 512], start=True, stop=True))(bb_, par), s_pe, 1)
        iz_free[par] = tbc
        b.wait("vector", tbc)
        tcp = b.op("vector", (lambda bb_: lambda e: e.tensor_copy(zbc_sb, ps[bb_]))(bb_), s_dve, 1)
        bankfree[bb_] = tcp
        ab = u["ab"]
        b.wait("vector", afst_free[ab])
        tstt = b.op("vector", (lambda bo_, ab: lambda e: e.scalar_tensor_tensor(
            afst[ab], ps[bo_], 1.0, zbc_sb, ALU.mult, ALU.mult))(bo_, ab), s_dve, 1)
        bankfree[bo_] = tstt
        b.wait("sync", tstt)
        tsa = b.op("sync", (lambda u, ab: lambda e: e.dma_start(
            af_sc[u["h"], :, u["tq"] * 512:(u["tq"] + 1) * 512], afst[ab]))(u, ab), m_af[ab], 16)
        afst_free[ab] = tsa

    prev = None
    pend_evac = []
    for i in range(NH * 8):
        h, tq = i // 8, i % 8
        if i == 0:
            emit_head_loads(0)
        if tq == 1 and h + 1 < NH:
            emit_head_loads(h + 1)
        hb = h % 2
        par = i % 2
        nk = 4 * (tq + 1)
        u = {"h": h, "tq": tq, "hb": hb, "par": par, "nk": nk, "exp": {}, "ab": i % 2}
        pend_now = pend_evac.pop(0) if len(pend_evac) == 1 else None
        prev_nk = prev["nk"] if prev else 0
        pv_last = None
        for c in range(max(nk, prev_nk)):
            if c < nk:
                bk = c % 2
                b.wait("tensor", tmax([head_loads[hb]], [bankfree[bk]]))
                tsc = b.op("tensor", (lambda hb, c, tq, bk: lambda e: e.matmul(
                    ps[bk], kb[hb][:, c * 128:(c + 1) * 128], qb[hb][:, tq * 512:(tq + 1) * 512],
                    start=True, stop=True))(hb, c, tq, bk), s_pe, 1)
                b.wait("scalar", tmax([tsc], [pbuf_free[par]] if c == 0 else None))
                te = b.op("scalar", (lambda c, bk, par: lambda e: e.activation(
                    pbuf[par][:, c * 512:(c + 1) * 512], ps[bk], AF.Exp, bias=ESHIFT, scale=SCALE))(c, bk, par), s_act, 1)
                bankfree[bk] = te
                if c >= 4 * tq:
                    kk = c - 4 * tq
                    b.wait("gpsimd", te)
                    te = b.op("gpsimd", (lambda c, kk, par: lambda e: e.tensor_tensor(
                        pbuf[par][:, c * 512:(c + 1) * 512], pbuf[par][:, c * 512:(c + 1) * 512],
                        mask_sb[:, kk * 512:(kk + 1) * 512], ALU.mult))(c, kk, par), s_gp, 1)
                u["exp"][c] = te
            if prev is not None and c < prev_nk:
                pv_last = emit_pv_chunk(prev, c)
            if c == 0 and pend_now is not None:
                emit_evacs(pend_now)
                pend_now = None
        if pend_now is not None:
            emit_evacs(pend_now)
            pend_now = None
        if prev is not None:
            pbuf_free[prev["par"]] = pv_last
            if prev["tq"] == 7:
                pe_head_last[prev["hb"]] = pv_last
            emit_recip(prev, pv_last)
            pend_evac.append(prev)
        prev = u
    pv_last = None
    for c in range(prev["nk"]):
        pv_last = emit_pv_chunk(prev, c)
    pbuf_free[prev["par"]] = pv_last
    pe_head_last[prev["hb"]] = pv_last
    emit_recip(prev, pv_last)
    pend_evac.append(prev)
    for uu in pend_evac:
        emit_evacs(uu)

    tok_p2_sync = tmax([b.final(s) for s in m_af])
    tok_p2_pe = (s_pe, b.counts["s_pe"])
    tok_p2_act = (s_act, b.counts["s_act"])
    tok_p2_dve = (s_dve, b.counts["s_dve"])
    tok_dwprep = tmax(dwc_store)

    if PHASES < 3:
        return _finish(nc, b)

    afm = pbf[:, 0:16384]
    wo_sb = pbf[:, 16384:49152]
    l2st = pbf[:, 49152:57344]
    projs = [pf[:, 4096:6144], pf[:, 10240:12288]]
    yt = [pf[:, 6144:8192], pf[:, 8192:10240]]
    sqo = [6144, 8192]
    bobc_sb = pf[:, 12288:14336]
    yt_free = [tok_p2_dve, tok_p2_dve]

    twl = None
    for dc in range(NDC):
        b.wait("gpsimd", tmax([tok_p2_pe], [tok_p2_act], [tok_p2_dve], tok_dwprep))
        twl = b.op("gpsimd", (lambda dc: lambda e: e.dma_start(
            wo_sb[:, dc * 2048:(dc + 1) * 2048], wo_h[dc * 128:(dc + 1) * 128, :]))(dc), m_wo, 16)
    tok_wo = twl
    b.wait("sync", tmax([tok_p2_pe], [tok_p2_dve]))
    t_bo = b.op("sync", lambda e: e.dma_start(bobc_sb, bobc_h), m_const, 16)

    afm_free = tok_p2_pe
    l2st_free = None
    xt_free3 = [xt_free[0], xt_free[1]]
    proj_free = [None, None]
    l2_state = {"free": None}

    def emit_l2_transp(jj, p, tv2_tok):
        last = None
        proj = projs[jj % 2]
        for qd in range(4):
            bk = 4 + qd
            b.wait("tensor", tmax([tv2_tok], [bankfree[bk]]))
            for w in range(4):
                dc = qd * 4 + w
                tk2 = b.op("tensor", (lambda bk, w, dc, proj: lambda e: e.transpose(
                    ps[bk][:, w * 128:(w + 1) * 128], proj[:, dc * 128:(dc + 1) * 128], idf_sb))(bk, w, dc, proj),
                    s_pe if w == 3 else None, 1 if w == 3 else 0)
            last = tk2
            b.wait("vector", tmax([tk2], [l2_state["free"]] if qd == 0 and jj % 4 == 0 else None))
            dst = l2st.rearrange("p (hc t) -> p hc t", t=512)[:, qd * 4:(qd + 1) * 4, (jj % 4) * 128:((jj % 4) + 1) * 128]
            srcv = ps[bk].rearrange("p (w t) -> p w t", t=128)
            te2 = b.op("vector", (lambda dst, srcv: lambda e: e.tensor_copy(dst, srcv))(dst, srcv), s_dve, 1)
            bankfree[bk] = te2
        if jj % 4 == 3:
            b.wait("sync", te2)
            t0 = p * 1024 + (jj // 4) * 512
            fl = None
            for hc in range(NDC):
                fl = b.op("sync", (lambda hc, t0: lambda e: e.dma_start(
                    l2_sc[hc][:, t0:t0 + 512], l2st[:, hc * 512:(hc + 1) * 512]))(hc, t0), m_l2st, 16)
            l2_state["free"] = fl
        return last

    for p in range(4):
        b.wait("sync", tmax([afm_free], tok_p2_sync))
        tok_afm = None
        for hc in range(NDC):
            tok_afm = b.op("sync", (lambda hc, p: lambda e: e.dma_start(
                afm[:, hc * 1024:(hc + 1) * 1024], af_sc[hc][:, p * 1024:(p + 1) * 1024]))(hc, p), m_afm, 16)
        mm_last = None
        pend = None
        for j in range(8):
            tj = p * 8 + j
            xb = j % 2
            proj = projs[j % 2]
            b.wait("sync", xt_free3[xb])
            tlx = b.op("sync", (lambda tj, xb: lambda e: e.dma_start(xt[xb], x_h[tj * 128:(tj + 1) * 128, :]))(tj, xb), m_x[xb], 16)
            for o in range(4):
                bk = (j * 4 + o) % 4
                b.wait("tensor", tmax([tok_afm], [tok_wo], [bankfree[bk]]))
                for dc in range(NDC):
                    tk = b.op("tensor", (lambda dc, j, o, bk: lambda e: e.matmul(
                        ps[bk], afm[:, dc * 1024 + j * 128: dc * 1024 + (j + 1) * 128],
                        wo_sb[:, dc * 2048 + o * 512: dc * 2048 + (o + 1) * 512],
                        start=(dc == 0), stop=(dc == NDC - 1)))(dc, j, o, bk),
                        s_pe if dc == NDC - 1 else None, 1 if dc == NDC - 1 else 0)
                mm_last = tk
                b.wait("vector", tmax([tk], [t_bo], [proj_free[j % 2]] if o == 0 else None,
                                      [yt_free[1 - xb]] if o == 0 else None))
                tvp = b.op("vector", (lambda o, bk, proj: lambda e: e.scalar_tensor_tensor(
                    proj[:, o * 512:(o + 1) * 512], ps[bk], 1.0, bobc_sb[:, o * 512:(o + 1) * 512],
                    ALU.mult, ALU.add, accum_out=st_sb[:, 12 + o:13 + o]))(o, bk, proj), s_dve, 1)
                bankfree[bk] = tvp
            sq_t = pf[:, sqo[1 - xb]:sqo[1 - xb] + D]
            b.wait_own("vector", tvp)
            t_c1 = b.op("vector", lambda e: e.tensor_tensor(st_sb[:, 12:13], st_sb[:, 12:13], st_sb[:, 13:14], ALU.add), s_dve, 1)
            t_c2 = b.op("vector", lambda e: e.tensor_tensor(st_sb[:, 14:15], st_sb[:, 14:15], st_sb[:, 15:16], ALU.add), s_dve, 1)
            t_sq = b.op("vector", (lambda proj, sq_t: lambda e: e.scalar_tensor_tensor(
                sq_t, proj, 1.0, proj, ALU.mult, ALU.mult, accum_out=st_sb[:, 5:6]))(proj, sq_t), s_dve, 1)
            b.wait_own("vector", t_sq)
            t_c3 = b.op("vector", lambda e: e.tensor_tensor(st_sb[:, 4:5], st_sb[:, 12:13], st_sb[:, 14:15], ALU.add), s_dve, 1)
            t_v1 = b.op("vector", lambda e: e.tensor_scalar(st_sb[:, 5:6], st_sb[:, 5:6], 1.0 / D, EPS, ALU.mult, ALU.add), s_dve, 1)
            b.wait_own("vector", t_c3)
            t_m = b.op("vector", lambda e: e.tensor_scalar(st_sb[:, 4:5], st_sb[:, 4:5], 1.0 / D, 0.0, ALU.mult, ALU.add), s_dve, 1)
            b.wait_own("vector", t_m)
            t_m2 = b.op("vector", lambda e: e.tensor_tensor(st_sb[:, 6:7], st_sb[:, 4:5], st_sb[:, 4:5], ALU.mult), s_dve, 1)
            b.op("vector", (lambda proj: lambda e: e.tensor_scalar(proj, proj, st_sb[:, 4:5], 0.0, ALU.subtract, ALU.add))(proj))
            b.wait_own("vector", t_m2)
            t_var = b.op("vector", lambda e: e.tensor_tensor(st_sb[:, 5:6], st_sb[:, 5:6], st_sb[:, 6:7], ALU.subtract), s_dve, 1)
            b.wait("scalar", t_var)
            t_l1 = b.op("scalar", lambda e: e.activation(st_sb[:, 6:7], st_sb[:, 5:6], AF.Ln, bias=0.0, scale=np.float32(1.0)), s_act, 1)
            b.wait_own("scalar", t_l1)
            ta = b.op("scalar", lambda e: e.activation(st_sb[:, 7:8], st_sb[:, 6:7], AF.Exp, bias=0.0, scale=np.float32(-0.5)), s_act, 1)
            b.wait("vector", tmax([ta], [tlx], [yt_free[xb]]))
            ty = b.op("vector", (lambda xb, proj: lambda e: e.scalar_tensor_tensor(
                yt[xb], proj, st_sb[:, 7:8], xt[xb], ALU.mult, ALU.add,
                accum_out=st_sb[:, 8:9]))(xb, proj), s_dve, 1)
            xt_free3[xb] = ty
            b.wait("sync", ty)
            tsy = b.op("sync", (lambda tj, xb: lambda e: e.dma_start(
                out_h[tj * 128:(tj + 1) * 128, :], yt[xb]))(tj, xb), m_y[xb], 16)
            yt_free[xb] = tsy
            t_sq2 = b.op("vector", (lambda xb, sq_t: lambda e: e.scalar_tensor_tensor(
                sq_t, yt[xb], 1.0, yt[xb], ALU.mult, ALU.mult, accum_out=st_sb[:, 9:10]))(xb, sq_t), s_dve, 1)
            b.wait_own("vector", t_sq2)
            t_n1 = b.op("vector", lambda e: e.tensor_scalar(st_sb[:, 8:9], st_sb[:, 8:9], 1.0 / D, 0.0, ALU.mult, ALU.add), s_dve, 1)
            t_n2 = b.op("vector", lambda e: e.tensor_scalar(st_sb[:, 9:10], st_sb[:, 9:10], 1.0 / D, EPS, ALU.mult, ALU.add), s_dve, 1)
            b.wait_own("vector", t_n2)
            t_n3 = b.op("vector", lambda e: e.tensor_tensor(st_sb[:, 10:11], st_sb[:, 8:9], st_sb[:, 8:9], ALU.mult), s_dve, 1)
            b.wait_own("vector", t_n3)
            t_n4 = b.op("vector", lambda e: e.tensor_tensor(st_sb[:, 9:10], st_sb[:, 9:10], st_sb[:, 10:11], ALU.subtract), s_dve, 1)
            b.wait("scalar", t_n4)
            t_l2 = b.op("scalar", lambda e: e.activation(st_sb[:, 10:11], st_sb[:, 9:10], AF.Ln, bias=0.0, scale=np.float32(1.0)), s_act, 1)
            b.wait_own("scalar", t_l2)
            t_e2 = b.op("scalar", lambda e: e.activation(st_sb[:, 11:12], st_sb[:, 10:11], AF.Exp, bias=0.0, scale=np.float32(-0.5)), s_act, 1)
            b.wait("vector", t_e2)
            t_b = b.op("vector", lambda e: e.scalar_tensor_tensor(
                st_sb[:, 10:11], st_sb[:, 8:9], -1.0, st_sb[:, 11:12], ALU.mult, ALU.mult), s_dve, 1)
            b.wait_own("vector", t_b)
            tv2 = b.op("vector", (lambda xb, proj: lambda e: e.tensor_scalar(
                proj, yt[xb], st_sb[:, 11:12], st_sb[:, 10:11], ALU.mult, ALU.add))(xb, proj), s_dve, 1)
            if pend is not None:
                proj_free[pend[0] % 2] = emit_l2_transp(pend[0], p, pend[1])
            pend = (j, tv2)
        proj_free[pend[0] % 2] = emit_l2_transp(pend[0], p, pend[1])
        afm_free = mm_last
    l2st_free = l2_state["free"]

    tok_p3_ysync = tmax([b.final(s) for s in m_y])
    tok_p3_pe = (s_pe, b.counts["s_pe"])
    tok_p3_dve = (s_dve, b.counts["s_dve"])
    tok_p3_act = (s_act, b.counts["s_act"])

    if PHASES < 4:
        return _finish(nc, b)

    hdn = pbf[:, 0:45056]
    ln2fm = pbf[:, 45056:61440]
    gwp = [pbf[:, 61440:63488], pbf[:, 63488:65536]]
    uwp = [pbf[:, 65536:67584], pbf[:, 67584:69632]]
    dwp = [pbf[:, 69632 + i * 512: 69632 + (i + 1) * 512] for i in range(4)]
    silu_sb = [pf[:, i * 512:(i + 1) * 512] for i in range(4)]
    dsbc_sb = pf[:, 2048:4096]
    ffnst = [pf[:, 4096 + i * 512: 4096 + (i + 1) * 512] for i in range(4)]

    b.wait("sync", tmax([tok_p3_pe], [tok_p3_dve]))
    t_ds = b.op("sync", lambda e: e.dma_start(dsbc_sb, dsbc_h), m_const, 16)

    gwp_free = [tok_p3_pe, tok_p3_pe]
    uwp_free = [tok_p3_pe, tok_p3_pe]
    dwp_free = [None, None, None, None]
    ln2fm_free = tok_p3_pe
    ffnst_free = [tok_p3_pe] * 4
    hdn_free = None

    ft = [pf[:, 6144:8192], pf[:, 8192:10240]]
    ft_free = [tok_p3_dve, tok_p3_dve]
    ffn_done_tok = [None]

    def emit_epilogue_tile(pp, j):
        g = pp * 8 + j
        fb = j % 2
        b.wait("sync", tmax([ft_free[fb]], [b.final(s) for s in m_ffn] if j % 2 == 0 else None,
                            tok_p3_ysync if pp == 0 and j == 0 else None))
        tf = b.op("sync", (lambda g, fb: lambda e: e.dma_start(ft[fb], ffn_sc[g * 128:(g + 1) * 128, :]))(g, fb), m_ft[fb], 16)
        b.wait("vector", tf)
        ta5 = emit_ln(12, ft[fb], 10240)
        b.wait("vector", ta5)
        tapp = b.op("vector", (lambda fb: lambda e: e.tensor_scalar(ft[fb], ft[fb], st_sb[:, 15:16], 0.0, ALU.mult, ALU.add))(fb), s_dve, 1)
        b.wait("gpsimd", tmax([tapp], tok_p3_ysync if pp == 0 and j == 0 else None))
        tacc = b.op("gpsimd", (lambda g, fb: lambda e: e.dma_start(
            out_h[g * 128:(g + 1) * 128, :], ft[fb], accum_op=ALU.add))(g, fb), m_out[fb], 16)
        ft_free[fb] = tacc

    for p in range(4):
        b.wait("sync", tmax([ln2fm_free], [l2st_free]))
        tok_l2 = None
        for hc in range(NDC):
            tok_l2 = b.op("sync", (lambda hc, p: lambda e: e.dma_start(
                ln2fm[:, hc * 1024:(hc + 1) * 1024], l2_sc[hc][:, p * 1024:(p + 1) * 1024]))(hc, p), m_l2f, 16)
        hdn_last = None
        gu_last = None
        for ip in range(NIC):
            wb = ip % 2
            b.wait("gpsimd", gwp_free[wb])
            tg = b.op("gpsimd", (lambda ip, wb: lambda e: e.dma_start(
                gwp[wb].rearrange("p (hc i) -> p hc i", i=128), gw_h[ip]))(ip, wb), m_gw[wb], 16)
            b.wait("gpsimd", uwp_free[wb])
            tu = b.op("gpsimd", (lambda ip, wb: lambda e: e.dma_start(
                uwp[wb].rearrange("p (hc i) -> p hc i", i=128), uw_h[ip]))(ip, wb), m_uw[wb], 16)
            bbase = (ip % 2) * 4
            b.wait("tensor", tmax([tg], [tu], [tok_l2], [hdn_free] if ip == 0 else None,
                                  [bankfree[bbase]], [bankfree[bbase + 1]],
                                  [bankfree[bbase + 2]], [bankfree[bbase + 3]]))
            tk = None
            for hc in range(NDC):
                stt_ = hc == 0
                sp = hc == NDC - 1
                for k in range(2):
                    b.op("tensor", (lambda wb, hc, k, bbase, stt_, sp: lambda e: e.matmul(
                        ps[bbase + k], gwp[wb][:, hc * 128:(hc + 1) * 128],
                        ln2fm[:, hc * 1024 + k * 512: hc * 1024 + (k + 1) * 512],
                        start=stt_, stop=sp))(wb, hc, k, bbase, stt_, sp))
                for k in range(2):
                    last = sp and k == 1
                    tk = b.op("tensor", (lambda wb, hc, k, bbase, stt_, sp: lambda e: e.matmul(
                        ps[bbase + 2 + k], uwp[wb][:, hc * 128:(hc + 1) * 128],
                        ln2fm[:, hc * 1024 + k * 512: hc * 1024 + (k + 1) * 512],
                        start=stt_, stop=sp))(wb, hc, k, bbase, stt_, sp),
                        s_pe if last else None, 1 if last else 0)
            gu_last = tk
            gwp_free[wb] = tk
            uwp_free[wb] = tk
            for k in range(2):
                sb4 = (ip % 2) * 2 + k
                b.wait("scalar", tk)
                tsl = b.op("scalar", (lambda sb4, bbase, k, ip: lambda e: e.activation(
                    silu_sb[sb4], ps[bbase + k], AF.Silu, scale=gs_sb[:, ip:ip + 1]))(sb4, bbase, k, ip), s_act, 1)
                bankfree[bbase + k] = tsl
                b.wait("vector", tsl)
                th = b.op("vector", (lambda sb4, bbase, k, ip: lambda e: e.scalar_tensor_tensor(
                    hdn[:, ip * 1024 + k * 512: ip * 1024 + (k + 1) * 512],
                    ps[bbase + 2 + k], us_sb[:, ip:ip + 1], silu_sb[sb4], ALU.mult, ALU.mult))(sb4, bbase, k, ip), s_dve, 1)
                bankfree[bbase + 2 + k] = th
                hdn_last = th
        tok_hdn = hdn_last
        ln2fm_free = gu_last
        if DBG and p == 0:
            b.wait("sync", tok_hdn)
            b.op("sync", lambda e: e.dma_start(hdn_dbg, hdn), m_dbg, 16)
        down_last = None
        for o in range(4):
            for ip in range(NIC):
                wb = ip % 4
                b.wait("scalar", tmax([dwp_free[wb]], tok_dwprep if p == 0 and o == 0 and ip < 4 else None))
                td = b.op("scalar", (lambda ip, o, wb: lambda e: e.dma_start(
                    dwp[wb], dwbf_sc[ip][:, o * 512:(o + 1) * 512]))(ip, o, wb), m_dwp[wb], 16)
                if ip == 0:
                    b.wait("tensor", tmax([tok_hdn], *[[bankfree[i]] for i in range(8)]))
                b.wait("tensor", td)
                for j in range(8):
                    last = (j == 7)
                    tk = b.op("tensor", (lambda wb, ip, j: lambda e: e.matmul(
                        ps[j], hdn[:, ip * 1024 + j * 128: ip * 1024 + (j + 1) * 128],
                        dwp[wb], start=(ip == 0), stop=(ip == NIC - 1)))(wb, ip, j),
                        s_pe if last else None, 1 if last else 0)
                dwp_free[wb] = tk
            down_last = tk
            for j in range(8):
                fb = j % 4
                b.wait("vector", tmax([down_last], [t_ds], [ffnst_free[fb]]))
                tvd = b.op("vector", (lambda j, o, fb: lambda e: e.tensor_tensor(
                    ffnst[fb], ps[j], dsbc_sb[:, o * 512:(o + 1) * 512], ALU.mult))(j, o, fb), s_dve, 1)
                bankfree[j] = tvd
                b.wait("sync", tvd)
                tj = p * 8 + j
                tsf = b.op("sync", (lambda tj, o, fb: lambda e: e.dma_start(
                    ffn_sc[tj * 128:(tj + 1) * 128, o * 512:(o + 1) * 512], ffnst[fb]))(tj, o, fb), m_ffn[fb], 16)
                ffnst_free[fb] = tsf
            if p > 0:
                emit_epilogue_tile(p - 1, 2 * o)
                emit_epilogue_tile(p - 1, 2 * o + 1)
        hdn_free = down_last
        ffn_done_tok[0] = tmax([b.final(s) for s in m_ffn])

    for j in range(8):
        emit_epilogue_tile(3, j)
    tok_p4_sync = tmax([b.final(s) for s in m_ffn])

    return _finish(nc, b)

def _finish(nc, b):
    for name in b.semnames:
        if b.own[name] is None and b.counts[name] > 0:
            b.q["sync"].append(("w", name, b.counts[name]))
    stack = contextlib.ExitStack()
    b.emit(nc, stack)
    stack.close()
    return nc


def stage_inputs(L0_attn_Wqkv, L0_attn_bqkv, L0_attn_Wo, L0_attn_bo,
                 L0_ffn_gate_w, L0_ffn_up_w, L0_ffn_down_w,
                 L0_ffn_gate_s, L0_ffn_up_s, L0_ffn_down_s):
    f32 = lambda a: np.ascontiguousarray(np.asarray(a, dtype=np.float32))
    Wqkv = f32(L0_attn_Wqkv)
    sh = {}
    sh["wqkv_sw"] = np.ascontiguousarray(Wqkv.reshape(NDC, 128, 12, 512).transpose(2, 1, 0, 3))
    bq = f32(L0_attn_bqkv)
    sh["bqkv_t"] = np.ascontiguousarray(bq.reshape(48, 128).T)
    sh["bv_bc"] = np.ascontiguousarray(np.broadcast_to(bq[4096:6144], (128, D)))
    sh["wo_n"] = f32(L0_attn_Wo)
    sh["bo_bc"] = np.ascontiguousarray(np.broadcast_to(f32(L0_attn_bo), (128, D)))
    sh["gw_sw"] = np.ascontiguousarray(f32(L0_ffn_gate_w).reshape(NIC, 128, NDC, 128).transpose(0, 3, 2, 1))
    sh["uw_sw"] = np.ascontiguousarray(f32(L0_ffn_up_w).reshape(NIC, 128, NDC, 128).transpose(0, 3, 2, 1))
    sh["dw_t"] = np.ascontiguousarray(f32(L0_ffn_down_w).T.reshape(NIC, 128, D))
    sh["gs_t"] = np.ascontiguousarray(f32(L0_ffn_gate_s).reshape(NIC, 128).T)
    sh["us_t"] = np.ascontiguousarray(f32(L0_ffn_up_s).reshape(NIC, 128).T)
    sh["ds_bc"] = np.ascontiguousarray(np.broadcast_to(f32(L0_ffn_down_s), (128, D)))
    sl, tl = np.arange(128)[:, None], np.arange(512)[None, :]
    masks = np.stack([(tl >= sl + 128 * k) for k in range(4)], 0).astype(ml_dtypes.bfloat16)
    sh["masks"] = np.ascontiguousarray(masks.transpose(1, 0, 2).reshape(128, 2048))
    sh["ident_f32"] = np.eye(128, dtype=np.float32)
    sh["ones_f32"] = np.ones((1, 128), dtype=np.float32)
    return sh


def kernel(L0_attn_x, L0_attn_Wqkv, L0_attn_bqkv, L0_attn_Wo, L0_attn_bo,
           L0_ffn_gate_w, L0_ffn_up_w, L0_ffn_down_w,
           L0_ffn_gate_s, L0_ffn_up_s, L0_ffn_down_s):
    x = np.ascontiguousarray(np.asarray(L0_attn_x, dtype=np.float32))
    sh = stage_inputs(L0_attn_Wqkv, L0_attn_bqkv, L0_attn_Wo, L0_attn_bo,
                      L0_ffn_gate_w, L0_ffn_up_w, L0_ffn_down_w,
                      L0_ffn_gate_s, L0_ffn_up_s, L0_ffn_down_s)
    nc = build_nc()
    in_maps = [dict(sh, x=np.ascontiguousarray(x[c])) for c in range(8)]
    res = run_bass_kernel_spmd(nc, in_maps, core_ids=list(range(8)))
    return np.stack([r["out"] for r in res.results], axis=0)
